# version 02
# baseline (speedup 1.0000x reference)
import os
import numpy as np

import concourse.bass as bass
import concourse.mybir as mybir
from concourse.ap import AP
from concourse.bass import IndirectOffsetOnAxis
from concourse.bass_utils import run_bass_kernel_spmd

F32 = mybir.dt.float32
BF16 = mybir.dt.bfloat16
FP8 = mybir.dt.float8e4
U32 = mybir.dt.uint32
AF = mybir.ActivationFunctionType
ALU = mybir.AluOpType
AX = mybir.AxisListType

B, S, D = 4, 4096, 1024
NT = S // 128
NOWN = NT // 2
DQK = 128
DFF = 4096
NEXP = 8
SEG = 512
NCOMP = NEXP * SEG
GLEN = 4352
ERF_A = 0.7978845608028654
ERF_B = -0.5641895835477563
BIG = 30000.0
EPS = 1e-5

STAGE = os.environ.get("MK_STAGE", "full")
ENGS = ("sync", "scalar", "vector", "tensor", "gpsimd")



def _bucket(d):
    import math
    nb, mx = 32, 128
    me = nb // 2
    if d < me:
        return d
    v = me + int(math.log(max(d, 1) / me) / math.log(mx / me) * (nb - me))
    return min(v, nb - 1)


def _make_M(h):
    M = np.zeros((33, GLEN), dtype=np.float32)
    sc = ERF_A * np.sqrt(128.0)
    for m in range(GLEN):
        d = 4095 + 128 * h - m
        if d < 0:
            M[32, m] = -BIG
        else:
            M[_bucket(d), m] = sc
            M[32, m] = ERF_B
    return M



class _Op:
    __slots__ = ("eng", "fn", "deps", "hard", "sem", "inc", "needed", "val")

    def __init__(self, eng, fn, deps, hard, sem, inc):
        self.eng, self.fn, self.sem, self.inc = eng, fn, sem, inc
        self.deps = [d for d in deps if d is not None]
        self.hard = [d for d in hard if d is not None]
        self.needed = False
        self.val = None


class Sched:
    def __init__(self):
        self.ops = []

    def add(self, eng, fn, deps=(), dma=False, sem=None, hard=()):
        if sem is None:
            sem = ("q_" + eng) if dma else eng
        op = _Op(eng, fn, deps, hard, sem, 16 if dma else 1)
        self.ops.append(op)
        return op

    def finalize(self):
        for op in self.ops:
            if op.inc == 16:
                op.needed = True
            for d in op.deps:
                if d.sem != op.eng:
                    d.needed = True
            for d in op.hard:
                d.needed = True
        cnt = {}
        for op in self.ops:
            if op.needed:
                cnt[op.sem] = cnt.get(op.sem, 0) + op.inc
                op.val = cnt[op.sem]

    def emit_engine(self, ename, eng, sems):
        high = {}
        for op in self.ops:
            if op.eng != ename:
                continue
            for d in op.deps + op.hard:
                if d.sem == ename and d not in op.hard:
                    continue
                assert d.val is not None, "dep on un-finalized token"
                if high.get(d.sem, 0) < d.val:
                    eng.wait_ge(sems[d.sem], d.val)
                    high[d.sem] = d.val
            ins = op.fn(eng)
            if op.needed:
                ins.then_inc(sems[op.sem], op.inc)


class Arena:
    def __init__(self, nc, base, limit):
        self.nc, self.off, self.limit = nc, base, limit
        self.n = 0

    def alloc(self, name, shape, dtype, align=64):
        per_part = int(np.prod(shape[1:])) * mybir.dt.size(dtype)
        off = (self.off + align - 1) // align * align
        assert off + per_part <= self.limit, (
            f"SBUF arena overflow at {name}: {off}+{per_part} > {self.limit}")
        self.off = off + per_part
        self.n += 1
        return self.nc.alloc_sbuf_tensor_at(f"{name}_{self.n}", list(shape), dtype,
                                            offset=off)

    def mark(self):
        return self.off

    def reset(self, off):
        self.off = off



def build_nc(stage):
    do_A1 = stage not in ("dbgG",)
    do_A2 = stage not in ("dbgG", "dbgA1")
    nc = bass.Bass("TRN2", target_bir_lowering=False, debug=False)
    sch = Sched()
    ar = Arena(nc, base=17 * 1024, limit=223 * 1024)

    def inp(name, shape, dtype=F32):
        return nc.dram_tensor(name, list(shape), dtype, kind="ExternalInput")

    x_row = inp("x_row", [S, D])
    x_own = inp("x_own", [2048, D])
    Wqk_i = inp("Wqk", [D, DQK])
    bqk_i = inp("bqk", [DQK, 1])
    gam0_i = inp("gam0", [DQK, 1]); bet0_i = inp("bet0", [DQK, 1])
    gam1_i = inp("gam1", [DQK, 1]); bet1_i = inp("bet1", [DQK, 1])
    Wv_a = inp("Wv_a", [D, D])
    bvbc_a = inp("bvbc_a", [128, D])
    rel_i = inp("rel", [128, 1])
    Mh_i = inp("Mh", [128, GLEN])
    identc = inp("identc", [128, 128])
    out_ext = nc.dram_tensor("out", [2048, D], F32, kind="ExternalOutput")

    g_dram = nc.dram_tensor("g_dram", [128, 128], F32, kind="Internal")
    x1_own = nc.dram_tensor("x1_own", [2048, D], F32, kind="Internal")

    ident = ar.alloc("ident", [128, 128], F32)
    small = ar.alloc("small", [128, 8], F32)
    bqk_sb = ar.alloc("bqk", [128, 1], F32)
    g0_sb = ar.alloc("g0", [128, 1], F32); b0_sb = ar.alloc("b0", [128, 1], F32)
    g1_sb = ar.alloc("g1", [128, 1], F32); b1a_sb = ar.alloc("b1a", [128, 1], F32)
    scc = ar.alloc("scc", [128, 2], F32)
    scr = ar.alloc("scr", [128, D], F32)

    PSB = [nc.alloc_psum_tensor(f"psb{i}", [128, 1024], F32) for i in range(4)]
    PS = [PSB[i // 2][:, (i % 2) * 512:(i % 2 + 1) * 512] for i in range(8)]
    PSW = [PSB[2], PSB[3]]

    t_ident = sch.add("sync", lambda e: e.dma_start(out=ident[:, :], in_=identc.ap()), dma=True)
    t_bqk = sch.add("sync", lambda e: e.dma_start(out=bqk_sb[:, :], in_=bqk_i.ap()), dma=True)
    t_g0 = sch.add("sync", lambda e: e.dma_start(out=g0_sb[:, :], in_=gam0_i.ap()), dma=True)
    t_b0 = sch.add("sync", lambda e: e.dma_start(out=b0_sb[:, :], in_=bet0_i.ap()), dma=True)
    t_g1 = sch.add("sync", lambda e: e.dma_start(out=g1_sb[:, :], in_=gam1_i.ap()), dma=True)
    t_b1 = sch.add("sync", lambda e: e.dma_start(out=b1a_sb[:, :], in_=bet1_i.ap()), dma=True)
    t_sc0 = sch.add("vector", lambda e: e.memset(scc[:, 0:1], ERF_A / 4096.0))
    t_sc1 = sch.add("vector", lambda e: e.memset(scc[:, 1:2], 0.5))

    p0m = ar.mark()
    M_sb = ar.alloc("M", [128, GLEN], F32)
    relx = ar.alloc("relx", [128, 1], F32)
    tT = ar.alloc("tT", [128, 34], F32)
    gT = ar.alloc("gT", [128, 128], F32)
    t_m = sch.add("sync", lambda e: e.dma_start(out=M_sb[:, :], in_=Mh_i.ap()), dma=True)
    t_r = sch.add("sync", lambda e: e.dma_start(out=relx[:, :], in_=rel_i.ap()), dma=True)
    t_r1 = sch.add("vector", lambda e: e.memset(relx[32:33, :], 1.0), deps=[t_r])
    mm = [sch.add("tensor",
                  (lambda c: lambda e: e.matmul(PS[0][:, c:c + 1],
                                                M_sb[:, c * 128:(c + 1) * 128],
                                                relx[:, :], start=True, stop=True))(c),
                  deps=[t_m, t_r, t_r1]) for c in range(34)]
    t_ev = sch.add("vector", lambda e: e.tensor_copy(tT[:, 0:34], PS[0][:, 0:34]), deps=mm)
    t_tr = sch.add("tensor", lambda e: e.transpose(PS[1][0:34, 0:128], tT[:, 0:34], ident[:, :]),
                   deps=[t_ev, t_ident])
    t_ev2b = sch.add("vector", lambda e: e.memset(gT[:, :], 0.0))
    t_ev2 = sch.add("vector", lambda e: e.tensor_copy(gT[0:34, :], PS[1][0:34, 0:128]),
                    deps=[t_tr], hard=[t_ev2b])
    t_g = sch.add("gpsimd", lambda e: e.dma_start(out=g_dram.ap(), in_=gT[:, :]),
                  deps=[t_ev2, t_ev2b], dma=True)
    p0_done = [t_ev, t_ev2, t_g]

    ar.reset(p0m)
    wqk_sb = ar.alloc("wqk", [128, 8 * 128], F32)
    wv_sb = ar.alloc("wv", [128, 8 * 512], F32)
    k_fm = ar.alloc("kfm", [128, S], F32)
    q_own = ar.alloc("qown", [128, 2048], F32)
    v_sb = ar.alloc("vsb", [128, NT * 1024], F32)
    xa = [ar.alloc(f"xa{i}", [128, D], F32) for i in range(2)]
    lnfm = ar.alloc("lnfm", [128, 8 * 128], F32)
    bvbc_sb = ar.alloc("bvbc", [128, D], F32)
    qk_sb = ar.alloc("qksb", [128, 128], F32)
    slab = [ar.alloc(f"slab{i}", [128, 128], F32) for i in range(3)]
    attn = [ar.alloc(f"attn{i}", [128, 128], F32) for i in range(2)]
    tmp1 = [ar.alloc(f"tmp1{i}", [128, 128], F32) for i in range(2)]
    x1sb = [ar.alloc(f"x1sb{i}", [128, D], F32) for i in range(2)]

    t_wqk = sch.add("sync", lambda e: e.dma_start(
        out=wqk_sb[:, :].rearrange("p (c m) -> p c m", m=128),
        in_=Wqk_i.ap().rearrange("(c p) m -> p c m", p=128)), deps=p0_done, dma=True)
    t_bv = sch.add("sync", lambda e: e.dma_start(out=bvbc_sb[:, :], in_=bvbc_a.ap()),
                   deps=p0_done, dma=True)

    st_ = dict(xa_rd=[None, None], lnfm_rd=[], ps67_rd=[None], ps2_rd=None,
               ps3_rd=None, qk_rd=None)

    def emit_ln(x_tile, dep_load, out_ap=None, out_dep=None):
        if out_ap is None:
            out_ap = x_tile[:, :]
        if out_dep is None:
            out_dep = []
        elif not isinstance(out_dep, list):
            out_dep = [out_dep]
        t1 = sch.add("vector", lambda e: e.reduce_sum(small[:, 0:1], x_tile[:, :], axis=AX.X),
                     deps=[dep_load])
        t2 = sch.add("vector", lambda e: e.tensor_scalar_mul(small[:, 0:1], small[:, 0:1],
                                                             -1.0 / D), hard=[t1])
        t3 = sch.add("scalar", lambda e: e.activation(scr[:, :], x_tile[:, :], AF.Square,
                                                      bias=small[:, 0:1],
                                                      accum_out=small[:, 1:2]),
                     deps=[t2])
        t4 = sch.add("vector", lambda e: e.tensor_scalar(small[:, 2:3], small[:, 1:2],
                                                         1.0 / D, EPS, ALU.mult, ALU.add),
                     deps=[t3])
        t5 = sch.add("scalar", lambda e: e.activation(small[:, 3:4], small[:, 2:3], AF.Sqrt),
                     deps=[t4])
        t6 = sch.add("vector", lambda e: e.reciprocal(small[:, 4:5], small[:, 3:4]), deps=[t5])
        t7 = sch.add("vector", lambda e: e.tensor_scalar(out_ap, x_tile[:, :],
                                                         small[:, 0:1], small[:, 4:5],
                                                         ALU.add, ALU.mult),
                     deps=out_dep, hard=[t6])
        return t7

    def emit_transpose(ln_tok, x_tile, dst=None, dst_rd=()):
        if dst is None:
            dst = lnfm
            dst_rd = st_["lnfm_rd"]
        tts = []
        for c in range(8):
            ps = PS[6] if c < 4 else PS[7]
            col = (c % 4) * 128
            tts.append(sch.add(
                "tensor",
                (lambda cc, pps, ccol: lambda e: e.transpose(
                    pps[:, ccol:ccol + 128], x_tile[:, cc * 128:(cc + 1) * 128],
                    ident[:, :]))(c, ps, col),
                deps=[ln_tok, t_ident, st_["ps67_rd"][0]]))
        ev0 = sch.add("vector", (lambda dd: lambda e: e.tensor_copy(
            dd[:, 0:512], PS[6][:, :]))(dst), deps=tts[:4] + list(dst_rd))
        ev1 = sch.add("vector", (lambda dd: lambda e: e.tensor_copy(
            dd[:, 512:1024], PS[7][:, :]))(dst), deps=tts[4:])
        st_["ps67_rd"][0] = ev1
        return [ev0, ev1]

    def emit_qk(evs, dst_ap, gain, boff, extra_deps=()):
        qmm = [sch.add(
            "tensor",
            (lambda kk: lambda e: e.matmul(
                PS[3][:, 0:128], wqk_sb[:, kk * 128:(kk + 1) * 128],
                lnfm[:, kk * 128:(kk + 1) * 128],
                start=(kk == 0), stop=(kk == 7)))(kc),
            deps=(evs + [t_wqk] + list(extra_deps) + ([st_["ps3_rd"]] if kc == 0 else [])))
            for kc in range(8)]
        t_qkA = sch.add("scalar", lambda e: e.activation(
            qk_sb[:, :], PS[3][:, 0:128], AF.Silu, bias=bqk_sb[:, :]),
            deps=qmm + [t_bqk, st_["qk_rd"]])
        st_["ps3_rd"] = t_qkA
        t_aff = sch.add("vector", lambda e: e.tensor_scalar(
            dst_ap, qk_sb[:, :], gain, boff, ALU.mult, ALU.add),
            deps=[t_qkA, t_g1, t_b1, t_g0, t_b0])
        st_["qk_rd"] = t_aff
        return qmm, t_aff

    st_dbg = {}
    v_tok = [None] * NT
    k_tok = [None] * NT
    for hv in ((0, 1) if do_A1 else ()):
        t_wv = sch.add("sync", (lambda hh: lambda e: e.dma_start(
            out=wv_sb[:, :].rearrange("p (c m) -> p c m", m=512),
            in_=Wv_a.ap()[:, hh * 512:(hh + 1) * 512].rearrange("(c p) m -> p c m", p=128)))(hv),
            deps=(v_tok[:] if hv else p0_done), dma=True)
        for t in range(NT):
            sl = t % 2
            t_load = sch.add("scalar", (lambda tt, ss: lambda e: e.dma_start(
                out=xa[ss][:, :], in_=x_row.ap()[tt * 128:(tt + 1) * 128, :]))(t, sl),
                deps=[st_["xa_rd"][sl], st_dbg.get("sm")]
                + (p0_done if (hv == 0 and t < 2) else []), dma=True)
            t_ln = emit_ln(xa[sl], t_load)
            if stage == "dbgA1" and hv == 0 and t == 0:
                st_dbg["ln"] = sch.add("gpsimd", lambda e: e.dma_start(
                    out=out_ext.ap()[896:1024, :], in_=xa[0][:, :]),
                    deps=[t_ln], dma=True)
                st_dbg["sm"] = sch.add("gpsimd", lambda e: e.dma_start(
                    out=out_ext.ap()[1024:1152, 0:8], in_=small[:, 0:8]),
                    deps=[t_ln], dma=True)
                t_ln = st_dbg["sm"]
            evs = emit_transpose(t_ln, xa[sl])
            st_["xa_rd"][sl] = evs[1]
            vmm = [sch.add(
                "tensor",
                (lambda kk: lambda e: e.matmul(
                    PS[2][:, :], lnfm[:, kk * 128:(kk + 1) * 128],
                    wv_sb[:, kk * 512:(kk + 1) * 512],
                    start=(kk == 0), stop=(kk == 7)))(kc),
                deps=(evs + [t_wv] + ([st_["ps2_rd"]] if kc == 0 else [])))
                for kc in range(8)]
            t_badd = sch.add("vector", (lambda hh: lambda e: e.tensor_add(
                PS[2][:, :], PS[2][:, :], bvbc_sb[:, hh * 512:(hh + 1) * 512]))(hv),
                deps=[vmm[-1], t_bv])
            t_vev = sch.add("scalar", (lambda tt, hh: lambda e: e.activation(
                v_sb[:, tt * 1024 + hh * 512: tt * 1024 + hh * 512 + 512],
                PS[2][:, :], AF.Silu))(t, hv), deps=[t_badd])
            st_["ps2_rd"] = t_vev
            v_tok[t] = t_vev
            st_["lnfm_rd"] = vmm[:]
            if hv == 0:
                qmm, t_k = emit_qk(evs, k_fm[:, t * 128:(t + 1) * 128], g1_sb[:, :],
                                   b1a_sb[:, :])
                k_tok[t] = t_k
                st_["lnfm_rd"] = vmm + qmm

    q_tok = [None] * NOWN
    for s in range(NOWN if do_A1 else 0):
        sl = s % 2
        t_load = sch.add("scalar", (lambda ss, sl_: lambda e: e.dma_start(
            out=xa[sl_][:, :], in_=x_own.ap()[ss * 128:(ss + 1) * 128, :]))(s, sl),
            deps=[st_["xa_rd"][sl]], dma=True)
        t_ln = emit_ln(xa[sl], t_load)
        evs = emit_transpose(t_ln, xa[sl])
        st_["xa_rd"][sl] = evs[1]
        qmm, t_q = emit_qk(evs, q_own[:, s * 128:(s + 1) * 128], g0_sb[:, :], b0_sb[:, :],
                           extra_deps=st_["lnfm_rd"])
        q_tok[s] = t_q
        st_["lnfm_rd"] = qmm[:]

    x1_toks = []
    tr_ = dict(attn_rd=[None, None], slab_rd=[None, None, None], tmp_rd=[None, None],
               score_ev=[None, None], x1_rd=[None, None], av_done=None)
    for s_ in range(NOWN if do_A2 else 0):
        n_j = 2 * s_ + 2
        av_last = None
        for j in range(n_j):
            par = j % 2
            sl3 = j % 3
            base = 3968 - (2 * s_ - j) * 128
            t_slab = sch.add("sync", (lambda bb, s3: lambda e: e.dma_start(
                out=slab[s3][:, :], in_=AP(g_dram, bb, [[1, 128], [1, 128]])))(base, sl3),
                deps=[t_g, tr_["slab_rd"][sl3]], dma=True)
            deps_sc = [k_tok[j], q_tok[s_], tr_["score_ev"][par]]
            t_scm = sch.add("tensor", (lambda jj, ss2, pp: lambda e: e.matmul(
                PS[4 + pp][:, 0:128], k_fm[:, jj * 128:(jj + 1) * 128],
                q_own[:, ss2 * 128:(ss2 + 1) * 128], start=True, stop=True))(j, s_, par),
                deps=deps_sc)
            t_e1 = sch.add("vector", (lambda pp, s3: lambda e: e.scalar_tensor_tensor(
                tmp1[pp][:, :], PS[4 + pp][:, 0:128], scc[:, 0:1],
                slab[s3][:, 127::-1], ALU.mult, ALU.add))(par, sl3),
                deps=[t_scm, t_slab, t_sc0, tr_["tmp_rd"][par]])
            tr_["score_ev"][par] = t_e1
            tr_["slab_rd"][sl3] = t_e1
            t_e2 = sch.add("scalar", (lambda pp: lambda e: e.activation(
                attn[pp][:, :], tmp1[pp][:, :], AF.Erf))(par),
                deps=[t_e1, tr_["attn_rd"][par]])
            tr_["tmp_rd"][par] = t_e2
            t_e3 = sch.add("vector", (lambda pp: lambda e: e.tensor_scalar(
                attn[pp][:, :], attn[pp][:, :], scc[:, 1:2], scc[:, 1:2],
                ALU.mult, ALU.add))(par), deps=[t_e2, t_sc1])
            av_deps0 = [t_e3, v_tok[j]]
            if s_ == 0 and j == 0:
                av_deps0 += [st_["ps2_rd"], st_["ps3_rd"], v_tok[NT - 1], q_tok[NOWN - 1]]
            if j == 0:
                av_deps0 += [tr_["av_done"]]
            av0 = sch.add("tensor", (lambda jj, pp: lambda e: e.matmul(
                PS[2][:, :], attn[pp][:, :], v_sb[:, jj * 1024:jj * 1024 + 512],
                start=(jj == 0), stop=(jj == n_j - 1)))(j, par), deps=av_deps0)
            av1 = sch.add("tensor", (lambda jj, pp: lambda e: e.matmul(
                PS[3][:, :], attn[pp][:, :], v_sb[:, jj * 1024 + 512:jj * 1024 + 1024],
                start=(jj == 0), stop=(jj == n_j - 1)))(j, par),
                deps=[t_e3, v_tok[j]])
            tr_["attn_rd"][par] = av1
            av_last = av1
        sl = s_ % 2
        t_xl = sch.add("scalar", (lambda ss2, sl_: lambda e: e.dma_start(
            out=xa[sl_][:, :], in_=x_own.ap()[ss2 * 128:(ss2 + 1) * 128, :]))(s_, sl),
            deps=[st_["xa_rd"][sl]], dma=True)
        e0 = sch.add("vector", (lambda sl_: lambda e: e.tensor_add(
            x1sb[sl_][:, 0:512], PS[2][:, :], xa[sl_][:, 0:512]))(sl),
            deps=[av_last, t_xl, tr_["x1_rd"][sl]])
        e1 = sch.add("vector", (lambda sl_: lambda e: e.tensor_add(
            x1sb[sl_][:, 512:1024], PS[3][:, :], xa[sl_][:, 512:1024]))(sl),
            deps=[av_last, t_xl, tr_["x1_rd"][sl]])
        st_["xa_rd"][sl] = e1
        tr_["av_done"] = e1
        dst = out_ext if stage == "x1" else x1_own
        t_st = sch.add("gpsimd", (lambda ss2, sl_, dd: lambda e: e.dma_start(
            out=dd.ap()[ss2 * 128:(ss2 + 1) * 128, :], in_=x1sb[sl_][:, :]))(s_, sl, dst),
            deps=[e0, e1], dma=True)
        tr_["x1_rd"][sl] = t_st
        x1_toks.append(t_st)

    do_F1 = stage in ("x2", "x3", "full", "dbgF", "dbgB", "dbgF2", "dbgF3", "fullnc", "fullb1", "fullb2", "fullr", "fulle1", "fulle2", "fulle4")
    do_EXB = stage in ("x3", "full", "dbgB", "fullnc", "fullb1", "fullb2", "fullr", "fulle1", "fulle2", "fulle4")
    do_F2 = stage in ("full", "dbgF2", "dbgF3", "fullnc", "fullb1", "fullb2", "fullr", "fulle1", "fulle2", "fulle4")

    f_toks = []
    if do_F1:
        gate_i = inp("gate", [D, NEXP])
        w1_i = inp("w1", [NEXP, D, DFF])
        b1r_i = inp("b1r", [128, NEXP * 32])
        w2_i = inp("w2", [NEXP, DFF, D])
        b2bc_i = inp("b2bc", [NEXP, 128, D])
        xcomp = nc.dram_tensor("xcomp", [NCOMP, D], F32, kind="Internal")
        ycomp = nc.dram_tensor("ycomp", [NCOMP, D], F32, kind="Internal")
        x2_own = nc.dram_tensor("x2_own", [2048, D], F32, kind="Internal")
        x3_own = nc.dram_tensor("x3_own", [2048, D], F32, kind="Internal")
        cc_in = nc.dram_tensor("cc_in", [2048, D], F32, kind="Internal")
        cc_out = nc.dram_tensor("cc_out", [2 * 2048, D], F32, kind="Internal",
                                addr_space="Local")

        a_done = [x1_toks[-1], tr_["attn_rd"][0], tr_["attn_rd"][1],
                  tr_["tmp_rd"][0], tr_["tmp_rd"][1],
                  tr_["score_ev"][0], tr_["score_ev"][1]]

        ar.reset(p0m)
        big1 = ar.alloc("big1", [128, 16384], F32)
        lnsb = [big1[:, c * 1024:(c + 1) * 1024] for c in range(16)]
        h_sb = big1
        acc = ar.alloc("acc", [128, 16384], F32)
        oh_all = ar.alloc("ohall", [128, 128], F32)
        w_all = ar.alloc("wall", [128, 16], F32)
        maskw = ar.alloc("maskw", [128, 128], F32)
        scr8 = ar.alloc("scr8", [128, 8], F32)
        ones1 = ar.alloc("ones1", [1, 128], F32)
        gate_sb = ar.alloc("gatesb", [128, 64], F32)
        b1r_sb = ar.alloc("b1rsb", [128, NEXP * 32], F32)
        xf = [ar.alloc(f"xf{i}", [128, D], F32) for i in range(2)]
        xe_fm = ar.alloc("xefm", [128, 8 * 512], F32)
        flnfm = xe_fm[:, 0:1024]
        w1sb = [ar.alloc(f"w1sb{i}", [128, 8 * 512], F32) for i in range(2)]
        w2sb = [ar.alloc(f"w2sb{i}", [128, 1024], F32) for i in range(2)]
        b2bc_sb = ar.alloc("b2bcsb", [128, D], F32)
        x2sb = [big1[:, 0:1024], big1[:, 1024:2048]]
        ln2sb = [big1[:, 2048:3072], big1[:, 3072:4096]]
        xfe = [big1[:, 4096:5120], big1[:, 5120:6144]]

        t_gate = sch.add("sync", lambda e: e.dma_start(
            out=gate_sb[:, :].rearrange("p (c m) -> p c m", m=8),
            in_=gate_i.ap().rearrange("(c p) m -> p c m", p=128)),
            deps=a_done, dma=True)
        t_b1r = sch.add("sync", lambda e: e.dma_start(out=b1r_sb[:, :], in_=b1r_i.ap()),
                        deps=a_done, dma=True)
        t_ones1 = sch.add("vector", lambda e: e.memset(ones1[:, :], 1.0), deps=a_done)

        def emit_ffn(call, src_dram, store_fn, first_deps, upto="all"):
            fr = dict(route_ev=[None, None], xf_rd=[None, None],
                      oh_toks=[], w_toks=[])
            ln_st = []
            for c in range(16):
                sl = c % 2
                par = c % 2
                t_load = sch.add("scalar", (lambda cc, ss: lambda e: e.dma_start(
                    out=xf[ss][:, :], in_=src_dram.ap()[cc * 128:(cc + 1) * 128, :]))(c, sl),
                    deps=[fr["xf_rd"][sl]] + (first_deps if c < 2 else []), dma=True)
                t_ln = emit_ln(xf[sl], t_load, out_ap=lnsb[c][:, :],
                               out_dep=(first_deps if c == 0 else None))
                evs = emit_transpose(t_ln, lnsb[c], dst=flnfm,
                                     dst_rd=st_.get("flnfm_rd", []))
                fr["xf_rd"][sl] = t_ln
                rmm = [sch.add(
                    "tensor",
                    (lambda kk, pp: lambda e: e.matmul(
                        PS[4 + pp][:, 0:8], flnfm[:, kk * 128:(kk + 1) * 128],
                        gate_sb[:, kk * 8:(kk + 1) * 8],
                        start=(kk == 0), stop=(kk == 7)))(kc, par),
                    deps=evs + [t_gate] + (fr["route_ev"][par] or [] if kc == 0 else []))
                    for kc in range(8)]
                st_["flnfm_rd"] = rmm
                t_mx = sch.add("vector", (lambda pp: lambda e: e.reduce_max(
                    small[:, 5:6], PS[4 + pp][:, 0:8], axis=AX.X))(par), deps=rmm)
                t_oh = sch.add("vector", (lambda cc, pp: lambda e: e.tensor_scalar(
                    oh_all[:, cc * 8:(cc + 1) * 8], PS[4 + pp][:, 0:8], small[:, 5:6],
                    None, ALU.is_ge))(c, par), hard=[t_mx])
                t_ng = sch.add("vector", lambda e: e.tensor_scalar_mul(
                    small[:, 6:7], small[:, 5:6], -1.0), hard=[t_mx])
                t_ex = sch.add("scalar", (lambda pp: lambda e: e.activation(
                    scr8[:, :], PS[4 + pp][:, 0:8], AF.Exp, bias=small[:, 6:7],
                    accum_out=small[:, 7:8]))(par), deps=[t_ng])
                fr["route_ev"][par] = [t_ex, t_oh]
                t_w = sch.add("vector", (lambda cc: lambda e: e.reciprocal(
                    w_all[:, cc:cc + 1], small[:, 7:8]))(c), deps=[t_ex])
                fr["oh_toks"].append(t_oh)
                fr["w_toks"].append(t_w)
                t_st = sch.add("gpsimd", (lambda cc: lambda e: e.dma_start(
                    out=xcomp.ap()[cc * 128:(cc + 1) * 128, :], in_=lnsb[cc][:, :]))(c),
                    deps=[t_ln], dma=True)
                ln_st.append(t_st)

            mw3 = maskw[:, :].rearrange("p (c e) -> p c e", e=8)
            oh3 = oh_all[:, :].rearrange("p (c e) -> p c e", e=8)
            mw_toks = []
            for e8 in range(8):
                mw_toks.append(sch.add("vector", (lambda ee: lambda e: e.tensor_mul(
                    mw3[:, :, ee], oh3[:, :, ee], w_all[:, :]))(e8),
                    hard=fr["oh_toks"] + fr["w_toks"]))

            if upto == "route":
                return ln_st + mw_toks
            ps_y_rd = {0: t_ev, 1: t_ev2}
            ps_h_rd = [None, None]
            prev_hmm = []
            prev_ymm = []
            w1_rd = [None, None]
            w2_rd = [None, None]
            b2_rd = None
            acc_rd = {}
            comb_toks = {}
            for e_ in range({"e1": 1, "e2": 2, "e4": 4}.get(upto, NEXP)):
                t_b2 = sch.add("sync", (lambda ee: lambda e: e.dma_start(
                    out=b2bc_sb[:, :], in_=b2bc_i.ap()[ee]))(e_),
                    deps=[b2_rd] + (first_deps if e_ == 0 else []), dma=True)
                for grp in range(4):
                    xe_evs = []
                    for tc in range(4):
                        sl = tc % 2
                        t_xe = sch.add("scalar", (lambda gg, tt, ss: lambda e: e.dma_start(
                            out=xf[ss][:, :],
                            in_=xcomp.ap()[gg * 512 + tt * 128: gg * 512 + (tt + 1) * 128, :]))(grp, tc, sl),
                            deps=ln_st + [fr["xf_rd"][sl]], dma=True)
                        tts = []
                        for cc in range(8):
                            ps = PS[6] if cc < 4 else PS[7]
                            col = (cc % 4) * 128
                            tts.append(sch.add(
                                "tensor",
                                (lambda c2, pps, ccol2, ss: lambda e: e.transpose(
                                    pps[:, ccol2:ccol2 + 128], xf[ss][:, c2 * 128:(c2 + 1) * 128],
                                    ident[:, :]))(cc, ps, col, sl),
                                deps=[t_xe, st_["ps67_rd"][0]]))
                        dst0 = xe_fm[:, :].rearrange("p (c m) -> p c m", m=512)[:, 0:4, tc * 128:tc * 128 + 128]
                        dst1 = xe_fm[:, :].rearrange("p (c m) -> p c m", m=512)[:, 4:8, tc * 128:tc * 128 + 128]
                        ev0 = sch.add("vector", (lambda dd: lambda e: e.tensor_copy(
                            dd, PS[6][:, :].rearrange("p (c m) -> p c m", m=128)))(dst0),
                            deps=tts[:4] + ([prev_hmm[-1]] if prev_hmm else [])
                            + (st_["flnfm_rd"] if (e_ == 0 and grp == 0 and tc == 0) else []))
                        ev1 = sch.add("vector", (lambda dd: lambda e: e.tensor_copy(
                            dd, PS[7][:, :].rearrange("p (c m) -> p c m", m=128)))(dst1),
                            deps=tts[4:])
                        st_["ps67_rd"][0] = ev1
                        fr["xf_rd"][sl] = ev1
                        xe_evs += [ev0, ev1]
                    h_toks = []
                    hmm_all = []
                    for ffs in range(8):
                        sl2 = ffs % 2
                        t_w1 = sch.add("sync", (lambda ee, ff, ss: lambda e: e.dma_start(
                            out=w1sb[ss][:, :].rearrange("p (c m) -> p c m", m=512),
                            in_=w1_i.ap()[ee, :, ff * 512:(ff + 1) * 512]
                            .rearrange("(c p) m -> p c m", p=128)))(e_, ffs, sl2),
                            deps=[w1_rd[sl2]]
                            + (first_deps if (e_ == 0 and grp == 0 and ffs < 2) else []),
                            dma=True)
                        for fl in range(4):
                            ffc = ffs * 4 + fl
                            pp = ffc % 2
                            hmm = [sch.add(
                                "tensor",
                                (lambda kk, ss, ffl, p2: lambda e: e.matmul(
                                    PS[4 + p2][:, :],
                                    w1sb[ss][:, kk * 512 + ffl * 128: kk * 512 + ffl * 128 + 128],
                                    xe_fm[:, kk * 512:(kk + 1) * 512],
                                    start=(kk == 0), stop=(kk == 7)))(kc, sl2, fl, pp),
                                deps=xe_evs + [t_w1]
                                + (([ps_h_rd[pp], ps_y_rd.get(4 + pp)]
                                    + (fr["route_ev"][pp] or [])) if kc == 0 else []))
                                for kc in range(8)]
                            hmm_all += hmm
                            t_h = sch.add("scalar", (lambda ffc2, p2: lambda e: e.activation(
                                h_sb[:, ffc2 * 512:(ffc2 + 1) * 512], PS[4 + p2][:, :],
                                AF.Gelu,
                                bias=b1r_sb[:, e_ * 32 + ffc2: e_ * 32 + ffc2 + 1]))(ffc, pp),
                                deps=hmm + (ln_st if (e_ == 0 and grp == 0) else [])
                                + ([prev_ymm[-1]] if prev_ymm else []))
                            ps_h_rd[pp] = t_h
                            h_toks.append(t_h)
                        w1_rd[sl2] = hmm_all[-1]
                    prev_hmm = hmm_all
                    ymm_last = []
                    for ffc in range(32):
                        sl2 = ffc % 2
                        t_w2 = sch.add("sync", (lambda ee, ff, ss: lambda e: e.dma_start(
                            out=w2sb[ss][:, :],
                            in_=w2_i.ap()[ee, ff * 128:(ff + 1) * 128, :]))(e_, ffc, sl2),
                            deps=[w2_rd[sl2]]
                            + (first_deps if (e_ == 0 and grp == 0 and ffc < 2) else []),
                            dma=True)
                        ymm = []
                        for tc in range(4):
                            for half in range(2):
                                bk = 2 * tc + half
                                extra = []
                                if ffc == 0:
                                    extra = [ps_y_rd.get(bk)]
                                    if bk in (4, 5):
                                        extra.append(ps_h_rd[bk - 4])
                                ymm.append(sch.add(
                                    "tensor",
                                    (lambda ffc2, tt, hh, ss, b2k: lambda e: e.matmul(
                                        PS[b2k][:, :],
                                        h_sb[:, ffc2 * 512 + tt * 128: ffc2 * 512 + tt * 128 + 128],
                                        w2sb[ss][:, hh * 512:(hh + 1) * 512],
                                        start=(ffc2 == 0), stop=False))(ffc, tc, half, sl2, bk),
                                    deps=[t_w2, h_toks[ffc]] + extra))
                        w2_rd[sl2] = ymm[-1]
                        ymm_last = ymm
                    bias_mm = []
                    for tc in range(4):
                        for half in range(2):
                            bk = 2 * tc + half
                            bias_mm.append(sch.add(
                                "tensor",
                                (lambda hh, b2k: lambda e: e.matmul(
                                    PS[b2k][:, :], ones1[:, :],
                                    b2bc_sb[0:1, hh * 512:(hh + 1) * 512],
                                    start=False, stop=True))(half, bk),
                                deps=[t_b2, t_ones1, ymm_last[-1]]))
                    prev_ymm = bias_mm
                    for tc in range(4):
                        c = grp * 4 + tc
                        for half in range(2):
                            bk = 2 * tc + half
                            mcol = maskw[:, c * 8 + e_: c * 8 + e_ + 1]
                            a_ap = acc[:, c * 1024 + half * 512: c * 1024 + half * 512 + 512]
                            if e_ == 0:
                                t_cb = sch.add("vector",
                                               (lambda aa, mm2, b2k: lambda e: e.tensor_scalar(
                                                   aa, PS[b2k][:, :], mm2, None, ALU.mult))(a_ap, mcol, bk),
                                               deps=[bias_mm[bk]] + mw_toks)
                            else:
                                t_cb = sch.add("vector",
                                               (lambda aa, mm2, b2k: lambda e: e.scalar_tensor_tensor(
                                                   aa, PS[b2k][:, :], mm2, aa,
                                                   ALU.mult, ALU.add))(a_ap, mcol, bk),
                                               deps=[bias_mm[bk], comb_toks[(c, half)]])
                            comb_toks[(c, half)] = t_cb
                            ps_y_rd[bk] = t_cb
                            if bk >= 6:
                                st_["ps67_rd"][0] = t_cb
                b2_rd = t_b2

            if upto in ("e1", "e2", "e4"):
                return [comb_toks[(c, h)] for c in range(16) for h in range(2)]
            out_toks = []
            st_out_rd = [[], []]
            for c in range(16):
                sl = c % 2
                t_x2l = sch.add("scalar", (lambda cc, ss: lambda e: e.dma_start(
                    out=xfe[ss], in_=src_dram.ap()[cc * 128:(cc + 1) * 128, :]))(c, sl),
                    deps=[comb_toks[(c, 0)], comb_toks[(c, 1)],
                          prev_ymm[-1]] + st_out_rd[sl], dma=True)
                t_res = sch.add("vector", (lambda cc, ss: lambda e: e.tensor_add(
                    x2sb[ss], acc[:, cc * 1024:(cc + 1) * 1024], xfe[ss]))(c, sl),
                    deps=[t_x2l, comb_toks[(c, 0)], comb_toks[(c, 1)]])
                toks = store_fn(c, sl, t_res)
                st_out_rd[sl] = list(toks) + [t_res]
                out_toks += toks
            return out_toks

        x2sb_rd = [None, None]

        def store1(c, sl, t_res):
            toks = []
            dst1 = out_ext if stage in ("x2", "dbgF") else x2_own
            t_s1 = sch.add("gpsimd", (lambda cc, ss, dd: lambda e: e.dma_start(
                out=dd.ap()[cc * 128:(cc + 1) * 128, :], in_=x2sb[ss]))(c, sl, dst1),
                deps=[t_res], dma=True)
            toks.append(t_s1)
            if do_EXB:
                t_ln2 = emit_ln(x2sb[sl], t_res, out_ap=ln2sb[sl],
                                out_dep=x2sb_rd[sl])
                t_s2 = sch.add("gpsimd", (lambda cc, ss: lambda e: e.dma_start(
                    out=cc_in.ap()[cc * 128:(cc + 1) * 128, :], in_=ln2sb[ss]))(c, sl),
                    deps=[t_ln2], dma=True)
                x2sb_rd[sl] = t_s2
                toks.append(t_s2)
            return toks

        f_toks = emit_ffn(1, x1_own, store1, first_deps=a_done)

    if do_EXB:
        WqT_i = inp("WqT", [D, D])
        WkT_i = inp("WkT", [D, D])
        WvT_i = inp("WvT", [D, D])
        WoT_i = inp("WoT", [D, D])
        bqr_i = inp("bqr", [128, 8])
        bkr_i = inp("bkr", [128, 8])
        bor_i = inp("bor", [128, 8])
        bvbb_i = inp("bvbb", [128, D])
        outfm_dram = nc.dram_tensor("outfm", [D, 2048], F32, kind="Internal")

        GROUPS = [[0, 1], [2, 3], [4, 5], [6, 7]]
        if stage == "fullnc":
            t_cc0 = sch.add("gpsimd", lambda e: e.dma_start(
                out=cc_out.ap()[0:2048, :], in_=cc_in.ap()), deps=f_toks, dma=True)
            t_cc = sch.add("gpsimd", lambda e: e.dma_start(
                out=cc_out.ap()[2048:4096, :], in_=cc_in.ap()), deps=[t_cc0], dma=True)
        else:
            t_cc = sch.add("gpsimd", lambda e: e.collective_compute(
                "AllGather", ALU.bypass, replica_groups=GROUPS,
                ins=[cc_in.ap()], outs=[cc_out.ap()]),
                deps=f_toks, sem="cc")

        ar.reset(p0m)
        vb_sb = ar.alloc("vbsb", [128, NT * 512], F32)
        k8_sb = ar.alloc("k8sb", [128, 8 * S], FP8)
        q8_sb = ar.alloc("q8sb", [128, 8 * 2048], FP8)
        wvb_sb = ar.alloc("wvbsb", [128, 8 * 512], F32)
        wk_sb = [ar.alloc(f"wksb{i}", [128, 8 * 128], F32) for i in range(2)]
        x2fmb = ar.alloc("x2fmb", [128, 8 * 512], F32)
        xb = [ar.alloc(f"xb{i}", [128, D], F32) for i in range(2)]
        attn4 = ar.alloc("attn4", [128, 4 * 1024], F32)
        otr = [ar.alloc(f"otr{i}", [128, 512], F32) for i in range(2)]
        outsp = [ar.alloc(f"outsp{i}", [128, 512], F32) for i in range(2)]
        thr_sb = ar.alloc("thrsb", [128, 25], F32)
        bvbb_sb = ar.alloc("bvbbsb", [128, D], F32)

        b_first = list(f_toks)
        t_bq = sch.add("sync", lambda e: e.dma_start(out=thr_sb[:, 0:8], in_=bqr_i.ap()),
                       deps=b_first, dma=True)
        t_bk = sch.add("sync", lambda e: e.dma_start(out=thr_sb[:, 8:16], in_=bkr_i.ap()),
                       dma=True)
        t_bo = sch.add("sync", lambda e: e.dma_start(out=thr_sb[:, 16:24], in_=bor_i.ap()),
                       dma=True)
        t_bvb = sch.add("sync", lambda e: e.dma_start(out=bvbb_sb[:, :], in_=bvbb_i.ap()),
                        deps=b_first, dma=True)
        t_thr = sch.add("vector", lambda e: e.tensor_scalar(
            thr_sb[:, 0:24], thr_sb[:, 0:24], -1.0, 1.0, ALU.mult, ALU.add),
            deps=[t_bq, t_bk, t_bo] + b_first)
        t_ths = sch.add("vector", lambda e: e.memset(thr_sb[:, 24:25], 0.125),
                        deps=[t_thr], hard=[t_thr])

        def row_tile_src(t):
            r0 = (t % 2) * 2048 + (t // 2) * 128
            return cc_out.ap()[r0:r0 + 128, :]

        btr = dict(xb_rd=[None, None], x2fmb_rd=[], v_rd={}, k8_rd=None, q8_rd=None,
                   wk_rd=[None, None], att_rd=[None, None], sc_ev=[None, None],
                   sig_ev=[None, None], out_ps_rd=[None] * 4, outsp_rd=[None, None],
                   ofm_rd=[], wo_rd=[None, None], ops_rd=[None, None],
                   spike_tts=None, x2l_rd=[None, None])

        def emit_group_transpose(load_tok, sl, tloc, extra=()):
            tts = []
            for cc in range(8):
                ps = PS[6] if cc < 4 else PS[7]
                col = (cc % 4) * 128
                tts.append(sch.add(
                    "tensor",
                    (lambda c2, pps, ccol2, ss: lambda e: e.transpose(
                        pps[:, ccol2:ccol2 + 128], xb[ss][:, c2 * 128:(c2 + 1) * 128],
                        ident[:, :]))(cc, ps, col, sl),
                    deps=[load_tok, st_["ps67_rd"][0]]))
            dst0 = x2fmb[:, :].rearrange("p (c m) -> p c m", m=512)[:, 0:4, tloc * 128:tloc * 128 + 128]
            dst1 = x2fmb[:, :].rearrange("p (c m) -> p c m", m=512)[:, 4:8, tloc * 128:tloc * 128 + 128]
            ev0 = sch.add("vector", (lambda dd: lambda e: e.tensor_copy(
                dd, PS[6][:, :].rearrange("p (c m) -> p c m", m=128)))(dst0),
                deps=tts[:4] + btr["x2fmb_rd"] + list(extra))
            ev1 = sch.add("vector", (lambda dd: lambda e: e.tensor_copy(
                dd, PS[7][:, :].rearrange("p (c m) -> p c m", m=128)))(dst1),
                deps=tts[4:])
            st_["ps67_rd"][0] = ev1
            btr["xb_rd"][sl] = ev1
            return [ev0, ev1]

        for tg in range(4):
            evs_all = []
            old_rd = btr["x2fmb_rd"]
            for tloc in range(4):
                sl = tloc % 2
                t_l = sch.add("scalar", (lambda gg, tt, ss: lambda e: e.dma_start(
                    out=xb[ss][:, :],
                    in_=cc_in.ap()[gg * 512 + tt * 128: gg * 512 + (tt + 1) * 128, :]))(tg, tloc, sl),
                    deps=[btr["xb_rd"][sl]] + (b_first if (tg == 0 and tloc < 2) else []),
                    dma=True)
                evs_all += emit_group_transpose(t_l, sl, tloc,
                                                extra=(b_first if (tg == 0 and tloc == 0) else ()))
            btr["x2fmb_rd"] = []
            for ko in range(8):
                sl2 = ko % 2
                t_wq = sch.add("sync", (lambda kk, ss: lambda e: e.dma_start(
                    out=wk_sb[ss][:, :].rearrange("p (c m) -> p c m", m=128),
                    in_=WqT_i.ap()[:, kk * 128:(kk + 1) * 128]
                    .rearrange("(c p) m -> p c m", p=128)))(ko, sl2),
                    deps=[btr["wk_rd"][sl2]] + (b_first if (tg == 0 and ko < 2) else []),
                    dma=True)
                par = ko % 2
                qmm = [sch.add(
                    "tensor",
                    (lambda kk, ss, p2: lambda e: e.matmul(
                        PS[4 + p2][:, :], wk_sb[ss][:, kk * 128:(kk + 1) * 128],
                        x2fmb[:, kk * 512:(kk + 1) * 512],
                        start=(kk == 0), stop=(kk == 7)))(kc, sl2, par),
                    deps=evs_all + [t_wq] + ([btr["sc_ev"][par]] if kc == 0 else []))
                    for kc in range(8)]
                btr["x2fmb_rd"] += qmm
                t_sp = sch.add("vector", (lambda kk, gg, p2: lambda e: e.tensor_scalar(
                    q8_sb[:, kk * 2048 + gg * 512: kk * 2048 + (gg + 1) * 512],
                    PS[4 + p2][:, :], thr_sb[:, kk:kk + 1], None, ALU.is_ge))(ko, tg, par),
                    deps=qmm + [t_thr])
                btr["sc_ev"][par] = t_sp
                btr["wk_rd"][sl2] = qmm[-1]
                btr["q8_rd"] = t_sp

        ofm_st_toks = []
        for g in range(2):
            t_wv = sch.add("sync", (lambda gg: lambda e: e.dma_start(
                out=wvb_sb[:, :].rearrange("p (c m) -> p c m", m=512),
                in_=WvT_i.ap()[:, gg * 512:(gg + 1) * 512]
                .rearrange("(c p) m -> p c m", p=128)))(g),
                deps=list(btr["v_rd"].values()) + (b_first if g == 0 else []), dma=True)
            for tg in range(8):
                evs_all = []
                for tloc in range(4):
                    sl = tloc % 2
                    t_l = sch.add("scalar", (lambda t4, ss: lambda e: e.dma_start(
                        out=xb[ss][:, :], in_=row_tile_src(t4)))(tg * 4 + tloc, sl),
                        deps=[btr["xb_rd"][sl], t_cc], dma=True)
                    evs_all += emit_group_transpose(t_l, sl, tloc)
                btr["x2fmb_rd"] = []
                for tloc in range(4):
                    t4 = tg * 4 + tloc
                    par = tloc % 2
                    vmm = [sch.add(
                        "tensor",
                        (lambda kk, tt, p2: lambda e: e.matmul(
                            PS[4 + p2][:, :],
                            x2fmb[:, kk * 512 + tt * 128: kk * 512 + tt * 128 + 128],
                            wvb_sb[:, kk * 512:(kk + 1) * 512],
                            start=(kk == 0), stop=(kk == 7)))(kc, tloc, par),
                        deps=evs_all + [t_wv] + ([btr["sc_ev"][par]] if kc == 0 else []))
                        for kc in range(8)]
                    btr["x2fmb_rd"] += vmm
                    t_v = sch.add("vector", (lambda t5, gg, p2: lambda e: e.tensor_add(
                        vb_sb[:, t5 * 512:(t5 + 1) * 512], PS[4 + p2][:, :],
                        bvbb_sb[:, gg * 512:(gg + 1) * 512]))(t4, g, par),
                        deps=vmm + [t_bvb, btr["v_rd"].get(t4)])
                    btr["sc_ev"][par] = t_v
                    btr["v_rd"][t4] = t_v
                if g == 0:
                    for ko in range(8):
                        sl2 = ko % 2
                        t_wk = sch.add("sync", (lambda kk, ss: lambda e: e.dma_start(
                            out=wk_sb[ss][:, :].rearrange("p (c m) -> p c m", m=128),
                            in_=WkT_i.ap()[:, kk * 128:(kk + 1) * 128]
                            .rearrange("(c p) m -> p c m", p=128)))(ko, sl2),
                            deps=[btr["wk_rd"][sl2]], dma=True)
                        par = ko % 2
                        kmm = [sch.add(
                            "tensor",
                            (lambda kk, ss, p2: lambda e: e.matmul(
                                PS[4 + p2][:, :], wk_sb[ss][:, kk * 128:(kk + 1) * 128],
                                x2fmb[:, kk * 512:(kk + 1) * 512],
                                start=(kk == 0), stop=(kk == 7)))(kc, sl2, par),
                            deps=evs_all + [t_wk] + ([btr["sc_ev"][par]] if kc == 0 else []))
                            for kc in range(8)]
                        btr["x2fmb_rd"] += kmm
                        t_sp = sch.add("vector", (lambda kk, gg, p2: lambda e: e.tensor_scalar(
                            k8_sb[:, kk * S + gg * 512: kk * S + (gg + 1) * 512],
                            PS[4 + p2][:, :], thr_sb[:, 8 + kk:9 + kk], None,
                            ALU.is_ge))(ko, tg, par),
                            deps=kmm + [t_thr])
                        btr["sc_ev"][par] = t_sp
                        btr["wk_rd"][sl2] = kmm[-1]
                        btr["k8_rd"] = t_sp
            if stage == "dbgB" and g == 0:
                t_dv = sch.add("gpsimd", lambda e: e.dma_start(
                    out=out_ext.ap()[256:384, 0:512], in_=vb_sb[:, 0:512]),
                    deps=[btr["v_rd"][0]], dma=True)
                btr["v_rd"][0] = t_dv
            if stage == "fullb1":
                continue
            if "att4_rd" not in btr:
                btr["att4_rd"] = [None] * 4
                btr["psw_rd"] = [None, None]
                btr["outq_rd"] = [None] * 4
                btr["otr_rd"] = [None, None]
            for sg in range(4):
                av_lasts = [None] * 4
                for j in range(NT):
                    sig_toks = []
                    for hp_loc in range(4):
                        hp = 4 * g + hp_loc
                        par = hp_loc % 2
                        first_sc = (sg == 0 and j == 0 and hp_loc < 2)
                        smm = []
                        for a in range(2):
                            bp = a * 64
                            smm.append(sch.add(
                                "tensor",
                                (lambda hh, jj, qq, aa, bpp, p2: lambda e: e.matmul(
                                    PSW[p2][:, aa * 512:(aa + 1) * 512],
                                    k8_sb[bpp:bpp + 64,
                                          hh * S + jj * 128: hh * S + (jj + 1) * 128],
                                    q8_sb[bpp:bpp + 64,
                                          hh * 2048 + qq * 512: hh * 2048 + (qq + 1) * 512],
                                    start=True, stop=True,
                                    tile_position=(bpp, 0)))(hp, j, sg, a, bp, par),
                                deps=[btr["k8_rd"], btr["q8_rd"], btr["psw_rd"][par]]
                                + ([btr["sc_ev"][0], btr["sc_ev"][1]] if first_sc else [])))
                        t_sig = sch.add("scalar", (lambda hl, p2: lambda e: e.activation(
                            attn4[:, hl * 1024:(hl + 1) * 1024], PSW[p2][:, :], AF.Sigmoid,
                            scale=thr_sb[:, 24:25]))(hp_loc, par),
                            deps=smm + [t_ths, btr["att4_rd"][hp_loc]])
                        btr["psw_rd"][par] = t_sig
                        sig_toks.append(t_sig)
                        if stage == "dbgB" and g == 0 and sg == 0 and hp_loc == 0 and j == 0:
                            t_da = sch.add("gpsimd", lambda e: e.dma_start(
                                out=out_ext.ap()[768:896, 0:1024], in_=attn4[:, 0:1024]),
                                deps=[t_sig], dma=True)
                            btr["att4_rd"][0] = t_da
                    for qs_ in range(4):
                        for hd in range(8):
                            hl = hd // 2
                            a = hd % 2
                            av_deps = [sig_toks[hl], btr["v_rd"][j]]
                            if j == 0 and hd == 0:
                                av_deps.append(btr["outq_rd"][qs_])
                            t_av = sch.add(
                                "tensor",
                                (lambda jj, qv, hd2, hl2, aa: lambda e: e.matmul(
                                    PS[qv][:, hd2 * 64:(hd2 + 1) * 64],
                                    attn4[:, hl2 * 1024 + aa * 512 + qv * 128:
                                          hl2 * 1024 + aa * 512 + qv * 128 + 128],
                                    vb_sb[:, jj * 512 + hd2 * 64: jj * 512 + hd2 * 64 + 64],
                                    start=(jj == 0 and hd2 == 0),
                                    stop=(jj == NT - 1 and hd2 == 7)))(j, qs_, hd, hl, a),
                                deps=av_deps)
                            btr["att4_rd"][hl] = t_av
                            av_lasts[qs_] = t_av
                for qs_ in range(4):
                    osl = qs_ % 2
                    t_oe = sch.add("vector", (lambda ss, qv: lambda e: e.tensor_copy(
                        outsp[ss][:, :], PS[qv][:, :]))(osl, qs_),
                        deps=[av_lasts[3], btr["outsp_rd"][osl]])
                    btr["outq_rd"][qs_] = t_oe
                    if stage == "dbgB" and g == 0 and sg == 0 and qs_ == 0:
                        t_do = sch.add("gpsimd", lambda e: e.dma_start(
                            out=out_ext.ap()[896:1024, 0:512], in_=outsp[0][:, :]),
                            deps=[t_oe], dma=True)
                        btr["outsp_rd"][0] = t_do
                    tts2 = []
                    for b in range(4):
                        ps = PS[6] if b < 2 else PS[7]
                        col = (b % 2) * 128
                        tts2.append(sch.add(
                            "tensor",
                            (lambda bb, pps, cc2, ss: lambda e: e.transpose(
                                pps[:, cc2:cc2 + 128],
                                outsp[ss][:, bb * 128:(bb + 1) * 128],
                                ident[:, :]))(b, ps, col, osl),
                            deps=[t_oe, st_["ps67_rd"][0], btr["psw_rd"][1]]))
                    t_tr0 = sch.add("vector", (lambda ss: lambda e: e.tensor_copy(
                        otr[ss][:, 0:256], PS[6][:, 0:256]))(osl),
                        deps=tts2[:2] + [btr["otr_rd"][osl]])
                    t_tr1 = sch.add("vector", (lambda ss: lambda e: e.tensor_copy(
                        otr[ss][:, 256:512], PS[7][:, 0:256]))(osl), deps=tts2[2:])
                    st_["ps67_rd"][0] = t_tr1
                    btr["psw_rd"][1] = t_tr1
                    btr["outsp_rd"][osl] = t_tr1
                    st_toks2 = []
                    for b in range(4):
                        st_toks2.append(sch.add("gpsimd",
                                                (lambda gg, qq, qv, bb, ss: lambda e: e.dma_start(
                            out=outfm_dram.ap()[gg * 512 + bb * 128: gg * 512 + (bb + 1) * 128,
                                                qq * 512 + qv * 128: qq * 512 + qv * 128 + 128],
                            in_=otr[ss][:, bb * 128:(bb + 1) * 128]))(g, sg, qs_, b, osl),
                            deps=[t_tr0, t_tr1], dma=True))
                    btr["otr_rd"][osl] = st_toks2[-1]
                    ofm_st_toks += st_toks2
            btr["sc_ev"][0] = btr["psw_rd"][0]
            btr["sc_ev"][1] = btr["psw_rd"][1]

        if stage == "fullb1":
            b_toks = [btr["k8_rd"], btr["q8_rd"]] + list(btr["v_rd"].values())
        if stage == "fullb2":
            b_toks = list(ofm_st_toks)
        ar.reset(p0m)
        skip_b2 = stage in ("fullb1", "fullb2")
        ofm_sb = ar.alloc("ofmsb", [128, 8 * 512], F32)
        wo_sb = [ar.alloc(f"wosb{i}", [128, 8 * 128], F32) for i in range(2)]
        spikefm = ar.alloc("spikefm", [128, 8 * 512], F32)
        x3sb = [ar.alloc(f"x3sb{i}", [128, D], F32) for i in range(2)]
        x2l = [ar.alloc(f"x2l{i}", [128, D], F32) for i in range(2)]
        if not skip_b2:
            b_toks = []
        x3_store_rd = [None, None]
        for sg in range(4 if not skip_b2 else 0):
            ofm_lds = []
            for hh in range(8):
                ofm_lds.append(sch.add("sync", (lambda h2, qq: lambda e: e.dma_start(
                    out=ofm_sb[:, h2 * 512:(h2 + 1) * 512],
                    in_=outfm_dram.ap()[h2 * 128:(h2 + 1) * 128,
                                        qq * 512:(qq + 1) * 512]))(hh, sg),
                    deps=ofm_st_toks + btr["ofm_rd"] + (b_first if sg == 0 else []),
                    dma=True))
            btr["ofm_rd"] = []
            sp_toks = []
            for oc in range(8):
                sl2 = oc % 2
                t_wo = sch.add("sync", (lambda kk, ss: lambda e: e.dma_start(
                    out=wo_sb[ss][:, :].rearrange("p (c m) -> p c m", m=128),
                    in_=WoT_i.ap()[:, kk * 128:(kk + 1) * 128]
                    .rearrange("(c p) m -> p c m", p=128)))(oc, sl2),
                    deps=[btr["wo_rd"][sl2]]
                    + (ofm_st_toks if (sg == 0 and oc < 2) else []), dma=True)
                par = oc % 2
                first_o = (sg == 0 and oc < 2)
                omm = [sch.add(
                    "tensor",
                    (lambda kk, ss, p2: lambda e: e.matmul(
                        PS[p2][:, :], wo_sb[ss][:, kk * 128:(kk + 1) * 128],
                        ofm_sb[:, kk * 512:(kk + 1) * 512],
                        start=(kk == 0), stop=(kk == 7)))(kc, sl2, par),
                    deps=ofm_lds + [t_wo]
                    + (([btr["ops_rd"][par]] + (btr["out_ps_rd"] if first_o else []))
                       if kc == 0 else []))
                    for kc in range(8)]
                btr["ofm_rd"] += omm
                btr["wo_rd"][sl2] = omm[-1]
                t_spk = sch.add("vector", (lambda occ, p2: lambda e: e.tensor_scalar(
                    spikefm[:, occ * 512:(occ + 1) * 512], PS[p2][:, :],
                    thr_sb[:, 16 + occ:17 + occ], None, ALU.is_ge))(oc, par),
                    deps=omm + [t_thr, btr["spike_tts"]])
                btr["ops_rd"][par] = t_spk
                sp_toks.append(t_spk)
            last_tts = None
            for tc in range(4):
                sl = tc % 2
                c = sg * 4 + tc
                t_x2 = sch.add("scalar", (lambda cc, ss: lambda e: e.dma_start(
                    out=x2l[ss][:, :], in_=x2_own.ap()[cc * 128:(cc + 1) * 128, :]))(c, sl),
                    deps=[btr["x2l_rd"][sl]]
                    + (ofm_st_toks if (sg == 0 and tc < 2) else []), dma=True)
                tts = []
                for occ in range(8):
                    ps = PS[6] if occ < 4 else PS[7]
                    col = (occ % 4) * 128
                    tts.append(sch.add(
                        "tensor",
                        (lambda o2, pps, ccol2, t2: lambda e: e.transpose(
                            pps[:, ccol2:ccol2 + 128],
                            spikefm[:, o2 * 512 + t2 * 128: o2 * 512 + t2 * 128 + 128],
                            ident[:, :]))(occ, ps, col, tc),
                        deps=sp_toks + [st_["ps67_rd"][0]]))
                last_tts = tts[-1]
                e0 = sch.add("vector", (lambda ss: lambda e: e.tensor_copy(
                    x3sb[ss][:, 0:512], PS[6][:, :]))(sl),
                    deps=tts[:4] + [x3_store_rd[sl]])
                e1 = sch.add("vector", (lambda ss: lambda e: e.tensor_copy(
                    x3sb[ss][:, 512:1024], PS[7][:, :]))(sl), deps=tts[4:])
                st_["ps67_rd"][0] = e1
                e2 = sch.add("vector", (lambda ss: lambda e: e.tensor_add(
                    x3sb[ss][:, :], x3sb[ss][:, :], x2l[ss][:, :]))(sl),
                    deps=[e0, e1, t_x2])
                btr["x2l_rd"][sl] = e2
                dstB = out_ext if stage == "x3" else x3_own
                t_st3 = sch.add("gpsimd", (lambda cc, ss, dd: lambda e: e.dma_start(
                    out=dd.ap()[cc * 128:(cc + 1) * 128, :], in_=x3sb[ss][:, :]))(c, sl, dstB),
                    deps=[e2], dma=True)
                x3_store_rd[sl] = t_st3
                b_toks.append(t_st3)
            btr["spike_tts"] = last_tts
        if stage == "dbgB":
            def dumpb(row0, src_ap, deps, cols):
                b_toks.append(sch.add("gpsimd", (lambda r0, sap, cc2: lambda e: e.dma_start(
                    out=out_ext.ap()[r0:r0 + sap.shape[0], 0:cc2], in_=sap))(row0, src_ap, cols),
                    deps=deps, dma=True))
            dumpb(0, q8_sb[:, 0:1024], [btr["q8_rd"]], 1024)
            dumpb(128, k8_sb[:, 0:1024], [btr["k8_rd"]], 1024)
            dumpb(256, vb_sb[:, 0:512], [btr["v_rd"][0]], 512)
            b_toks.append(sch.add("gpsimd", lambda e: e.dma_start(
                out=out_ext.ap()[384:512, 0:1024],
                in_=outfm_dram.ap()[0:128, 0:1024]), deps=ofm_st_toks, dma=True))
            dumpb(512, spikefm[:, 0:1024], sp_toks, 1024)
            b_toks.append(sch.add("gpsimd", lambda e: e.dma_start(
                out=out_ext.ap()[640:768, 0:1024],
                in_=cc_out.ap()[2048:2176, 0:1024]), deps=[t_cc], dma=True))
            b_toks.append(sch.add("gpsimd", lambda e: e.dma_start(
                out=out_ext.ap()[1024:1152, 0:1024],
                in_=outfm_dram.ap()[128:256, 0:1024]), deps=ofm_st_toks, dma=True))
            b_toks.append(sch.add("gpsimd", lambda e: e.dma_start(
                out=out_ext.ap()[1152:1280, 0:1024],
                in_=outfm_dram.ap()[512:640, 0:1024]), deps=ofm_st_toks, dma=True))
        f_toks = b_toks

    if do_F2:
        def store2(c, sl, t_res):
            t_s = sch.add("gpsimd", (lambda cc, ss: lambda e: e.dma_start(
                out=out_ext.ap()[cc * 128:(cc + 1) * 128, :], in_=x2sb[ss]))(c, sl),
                deps=[t_res], dma=True)
            return [t_s]

        src2 = x3_own if do_EXB else x2_own
        upto2 = {"fullr": "route", "fulle1": "e1", "fulle2": "e2", "fulle4": "e4"}.get(stage, "all")
        f_toks = emit_ffn(2, src2, store2, first_deps=f_toks, upto=upto2)
        if stage == "dbgF3":
            f_toks = emit_ffn(3, x2_own, store2, first_deps=f_toks)

    final_deps = list(f_toks) if do_F1 else list(x1_toks)

    if stage in ("dbg1", "dbgG", "dbgA1"):
        def dump(row0, src_ap, deps):
            final_deps.append(sch.add("gpsimd", lambda e: e.dma_start(
                out=out_ext.ap()[row0:row0 + src_ap.shape[0], 0:src_ap.shape[1]],
                in_=src_ap), deps=deps, dma=True))
        dump(768, gT[:, :], [t_ev2])
        if do_A1:
            dump(0, v_sb[:, 0:1024], [v_tok[0]])
            dump(128, v_sb[:, 31 * 1024:32 * 1024], [v_tok[31]])
            dump(256, k_fm[:, 0:1024], k_tok[:8])
            dump(384, k_fm[:, 3072:4096], k_tok[24:32])
            dump(512, q_own[:, 0:1024], q_tok[:8])
            dump(640, q_own[:, 1024:2048], q_tok[8:])
        if do_A2:
            dump(896, attn[1][:, :], [tr_["attn_rd"][1]])
            dump(1024, slab[31 % 3][:, :], [tr_["slab_rd"][31 % 3]])
        if stage == "dbgA1":
            final_deps.extend([st_dbg["ln"], st_dbg["sm"]])

    sch.add("gpsimd", lambda e: e.nop(), deps=final_deps)

    sch.finalize()
    from contextlib import ExitStack
    sem_names = ["sync", "scalar", "vector", "tensor", "gpsimd",
                 "q_sync", "q_scalar", "q_vector", "q_gpsimd", "cc"]
    with ExitStack() as stk:
        sems = {n: stk.enter_context(nc.semaphore("s_" + n)) for n in sem_names}
        with nc.Block() as block:
            for en in ENGS:
                block.__getattribute__(en)(
                    (lambda name: lambda eng: sch.emit_engine(name, eng, sems))(en))
    return nc



_NC_CACHE = {}


def kernel(**inputs):
    stage = STAGE
    f32 = lambda a: np.ascontiguousarray(np.asarray(a), dtype=np.float32)

    x = f32(inputs["L0_attn_a_x"])
    Wqk = f32(inputs["L0_attn_a_Wqk"])
    bqk = f32(inputs["L0_attn_a_bqk"]).reshape(DQK, 1)
    gam = f32(inputs["L0_attn_a_gamma"])
    bet = f32(inputs["L0_attn_a_beta"])
    Wv = f32(inputs["L0_attn_a_Wv"])
    bv = f32(inputs["L0_attn_a_bv"])
    rel = np.zeros((128, 1), dtype=np.float32)
    rel[0:32] = f32(inputs["L0_attn_a_rel_emb"]).reshape(32, 1)

    ident = np.eye(128, dtype=np.float32)
    bvbc = np.ascontiguousarray(np.broadcast_to(bv[None, :], (128, D)), dtype=np.float32)
    Mh = []
    for h in range(2):
        m = np.zeros((128, GLEN), dtype=np.float32)
        m[0:33] = _make_M(h)
        Mh.append(m)

    gate = f32(inputs["L0_ffn_gate_w"])
    w1 = f32(inputs["L0_ffn_w1"])
    b1 = f32(inputs["L0_ffn_b1"])
    w2 = f32(inputs["L0_ffn_w2"])
    b2 = f32(inputs["L0_ffn_b2"])
    b1r = np.concatenate([np.ascontiguousarray(b1[e].reshape(32, 128).T)
                          for e in range(NEXP)], axis=1)
    b2bc = np.ascontiguousarray(np.broadcast_to(b2[:, None, :], (NEXP, 128, D)),
                                dtype=np.float32)

    WqT = np.ascontiguousarray(f32(inputs["L0_attn_b_Wq"]).T)
    WkT = np.ascontiguousarray(f32(inputs["L0_attn_b_Wk"]).T)
    WvT = np.ascontiguousarray(f32(inputs["L0_attn_b_Wv"]).T)
    WoT = np.ascontiguousarray(f32(inputs["L0_attn_b_Wo"]).T)
    bqr = np.ascontiguousarray(f32(inputs["L0_attn_b_bq"]).reshape(8, 128).T)
    bkr = np.ascontiguousarray(f32(inputs["L0_attn_b_bk"]).reshape(8, 128).T)
    bor = np.ascontiguousarray(f32(inputs["L0_attn_b_bo"]).reshape(8, 128).T)
    bvbb = np.ascontiguousarray(np.broadcast_to(f32(inputs["L0_attn_b_bv"])[None, :],
                                                (128, D)), dtype=np.float32)

    if stage not in _NC_CACHE:
        _NC_CACHE[stage] = build_nc(stage)
    nc = _NC_CACHE[stage]

    in_maps = []
    own_rows = {}
    for c in range(8):
        b, h = c // 2, c % 2
        rows = np.concatenate([np.arange((2 * s + h) * 128, (2 * s + h + 1) * 128)
                               for s in range(NOWN)])
        own_rows[c] = rows
        im = {
            "x_row": x[b],
            "x_own": np.ascontiguousarray(x[b][rows]),
            "Wqk": Wqk,
            "bqk": bqk,
            "gam0": gam[0].reshape(DQK, 1), "bet0": bet[0].reshape(DQK, 1),
            "gam1": gam[1].reshape(DQK, 1), "bet1": bet[1].reshape(DQK, 1),
            "Wv_a": Wv,
            "bvbc_a": bvbc,
            "rel": rel,
            "Mh": Mh[h],
            "identc": ident,
        }
        if STAGE in ("x2", "x3", "full", "dbgF", "dbgB", "dbgF2", "dbgF3", "fullnc", "fullb1", "fullb2", "fullr", "fulle1", "fulle2", "fulle4"):
            im.update({
                "gate": gate, "w1": w1, "b1r": b1r, "w2": w2, "b2bc": b2bc,
            })
        if STAGE in ("x3", "full", "dbgB", "fullnc", "fullb1", "fullb2", "fullr", "fulle1", "fulle2", "fulle4"):
            im.update({
                "WqT": WqT, "WkT": WkT, "WvT": WvT, "WoT": WoT,
                "bqr": bqr, "bkr": bkr, "bor": bor, "bvbb": bvbb,
            })
        in_maps.append(im)

    res = run_bass_kernel_spmd(nc, in_maps, core_ids=list(range(8)))

    out = np.zeros((B, S, D), dtype=np.float32)
    for c in range(8):
        b = c // 2
        out[b][own_rows[c]] = res.results[c]["out"]
    return out


# version 03 (final)
# speedup vs baseline: 2.2024x; 2.2024x over previous
import os
import numpy as np

import concourse.bass as bass
import concourse.mybir as mybir
from concourse.ap import AP
from concourse.bass import IndirectOffsetOnAxis
from concourse.bass_utils import run_bass_kernel_spmd

F32 = mybir.dt.float32
BF16 = mybir.dt.bfloat16
FP8 = mybir.dt.float8e4
U32 = mybir.dt.uint32
AF = mybir.ActivationFunctionType
ALU = mybir.AluOpType
AX = mybir.AxisListType

B, S, D = 4, 4096, 1024
NT = S // 128
NOWN = NT // 2
DQK = 128
DFF = 4096
NEXP = 8
SEG = 512
NCOMP = NEXP * SEG
GLEN = 4352
ERF_A = 0.7978845608028654
ERF_B = -0.5641895835477563
BIG = 30000.0
EPS = 1e-5

STAGE = os.environ.get("MK_STAGE", "full")
ENGS = ("sync", "scalar", "vector", "tensor", "gpsimd")



def _bucket(d):
    import math
    nb, mx = 32, 128
    me = nb // 2
    if d < me:
        return d
    v = me + int(math.log(max(d, 1) / me) / math.log(mx / me) * (nb - me))
    return min(v, nb - 1)


def _make_M(h):
    M = np.zeros((33, GLEN), dtype=np.float32)
    sc = ERF_A * np.sqrt(128.0)
    for m in range(GLEN):
        d = 4095 + 128 * h - m
        if d < 0:
            M[32, m] = -BIG
        else:
            M[_bucket(d), m] = sc
            M[32, m] = ERF_B
    return M



class _Op:
    __slots__ = ("eng", "fn", "deps", "hard", "sem", "inc", "needed", "val")

    def __init__(self, eng, fn, deps, hard, sem, inc):
        self.eng, self.fn, self.sem, self.inc = eng, fn, sem, inc
        self.deps = [d for d in deps if d is not None]
        self.hard = [d for d in hard if d is not None]
        self.needed = False
        self.val = None


class Sched:
    def __init__(self):
        self.ops = []

    def add(self, eng, fn, deps=(), dma=False, sem=None, hard=()):
        if sem is None:
            sem = ("q_" + eng) if dma else eng
        op = _Op(eng, fn, deps, hard, sem, 16 if dma else 1)
        self.ops.append(op)
        return op

    def finalize(self):
        for op in self.ops:
            if op.inc == 16:
                op.needed = True
            for d in op.deps:
                if d.sem != op.eng:
                    d.needed = True
            for d in op.hard:
                d.needed = True
        cnt = {}
        for op in self.ops:
            if op.needed:
                cnt[op.sem] = cnt.get(op.sem, 0) + op.inc
                op.val = cnt[op.sem]

    def emit_engine(self, ename, eng, sems):
        high = {}
        for op in self.ops:
            if op.eng != ename:
                continue
            for d in op.deps + op.hard:
                if d.sem == ename and d not in op.hard:
                    continue
                assert d.val is not None, "dep on un-finalized token"
                if high.get(d.sem, 0) < d.val:
                    eng.wait_ge(sems[d.sem], d.val)
                    high[d.sem] = d.val
            ins = op.fn(eng)
            if op.needed:
                ins.then_inc(sems[op.sem], op.inc)


class Arena:
    def __init__(self, nc, base, limit):
        self.nc, self.off, self.limit = nc, base, limit
        self.n = 0

    def alloc(self, name, shape, dtype, align=64):
        per_part = int(np.prod(shape[1:])) * mybir.dt.size(dtype)
        off = (self.off + align - 1) // align * align
        assert off + per_part <= self.limit, (
            f"SBUF arena overflow at {name}: {off}+{per_part} > {self.limit}")
        self.off = off + per_part
        self.n += 1
        return self.nc.alloc_sbuf_tensor_at(f"{name}_{self.n}", list(shape), dtype,
                                            offset=off)

    def mark(self):
        return self.off

    def reset(self, off):
        self.off = off



def build_nc(stage):
    do_A1 = stage not in ("dbgG",)
    do_A2 = stage not in ("dbgG", "dbgA1")
    nc = bass.Bass("TRN2", target_bir_lowering=False, debug=False)
    sch = Sched()
    ar = Arena(nc, base=17 * 1024, limit=223 * 1024)

    def inp(name, shape, dtype=F32):
        return nc.dram_tensor(name, list(shape), dtype, kind="ExternalInput")

    x_row = inp("x_row", [S, D])
    x_own = inp("x_own", [2048, D])
    Wqk_i = inp("Wqk", [D, DQK])
    bqk_i = inp("bqk", [DQK, 1])
    gam0_i = inp("gam0", [DQK, 1]); bet0_i = inp("bet0", [DQK, 1])
    gam1_i = inp("gam1", [DQK, 1]); bet1_i = inp("bet1", [DQK, 1])
    Wv_a = inp("Wv_a", [D, D])
    bvbc_a = inp("bvbc_a", [128, D])
    rel_i = inp("rel", [128, 1])
    Mh_i = inp("Mh", [128, GLEN])
    identc = inp("identc", [128, 128])
    out_ext = nc.dram_tensor("out", [2048, D], F32, kind="ExternalOutput")

    g_dram = nc.dram_tensor("g_dram", [128, 128], F32, kind="Internal")
    x1_own = nc.dram_tensor("x1_own", [2048, D], F32, kind="Internal")

    ident = ar.alloc("ident", [128, 128], F32)
    small = ar.alloc("small", [128, 8], F32)
    bqk_sb = ar.alloc("bqk", [128, 1], F32)
    g0_sb = ar.alloc("g0", [128, 1], F32); b0_sb = ar.alloc("b0", [128, 1], F32)
    g1_sb = ar.alloc("g1", [128, 1], F32); b1a_sb = ar.alloc("b1a", [128, 1], F32)
    scc = ar.alloc("scc", [128, 2], F32)
    scr = ar.alloc("scr", [128, D], F32)

    PSB = [nc.alloc_psum_tensor(f"psb{i}", [128, 1024], F32) for i in range(4)]
    PS = [PSB[i // 2][:, (i % 2) * 512:(i % 2 + 1) * 512] for i in range(8)]
    PSW = [PSB[2], PSB[3]]

    t_ident = sch.add("sync", lambda e: e.dma_start(out=ident[:, :], in_=identc.ap()), dma=True)
    t_bqk = sch.add("sync", lambda e: e.dma_start(out=bqk_sb[:, :], in_=bqk_i.ap()), dma=True)
    t_g0 = sch.add("sync", lambda e: e.dma_start(out=g0_sb[:, :], in_=gam0_i.ap()), dma=True)
    t_b0 = sch.add("sync", lambda e: e.dma_start(out=b0_sb[:, :], in_=bet0_i.ap()), dma=True)
    t_g1 = sch.add("sync", lambda e: e.dma_start(out=g1_sb[:, :], in_=gam1_i.ap()), dma=True)
    t_b1 = sch.add("sync", lambda e: e.dma_start(out=b1a_sb[:, :], in_=bet1_i.ap()), dma=True)
    t_sc0 = sch.add("vector", lambda e: e.memset(scc[:, 0:1], ERF_A / 4096.0))
    t_sc1 = sch.add("vector", lambda e: e.memset(scc[:, 1:2], 0.5))

    p0m = ar.mark()
    M_sb = ar.alloc("M", [128, GLEN], F32)
    relx = ar.alloc("relx", [128, 1], F32)
    tT = ar.alloc("tT", [128, 34], F32)
    gT = ar.alloc("gT", [128, 128], F32)
    t_m = sch.add("sync", lambda e: e.dma_start(out=M_sb[:, :], in_=Mh_i.ap()), dma=True)
    t_r = sch.add("sync", lambda e: e.dma_start(out=relx[:, :], in_=rel_i.ap()), dma=True)
    t_r1 = sch.add("vector", lambda e: e.memset(relx[32:33, :], 1.0), deps=[t_r])
    mm = [sch.add("tensor",
                  (lambda c: lambda e: e.matmul(PS[0][:, c:c + 1],
                                                M_sb[:, c * 128:(c + 1) * 128],
                                                relx[:, :], start=True, stop=True))(c),
                  deps=[t_m, t_r, t_r1]) for c in range(34)]
    t_ev = sch.add("vector", lambda e: e.tensor_copy(tT[:, 0:34], PS[0][:, 0:34]), deps=mm)
    t_tr = sch.add("tensor", lambda e: e.transpose(PS[1][0:34, 0:128], tT[:, 0:34], ident[:, :]),
                   deps=[t_ev, t_ident])
    t_ev2b = sch.add("vector", lambda e: e.memset(gT[:, :], 0.0))
    t_ev2 = sch.add("vector", lambda e: e.tensor_copy(gT[0:34, :], PS[1][0:34, 0:128]),
                    deps=[t_tr], hard=[t_ev2b])
    t_g = sch.add("gpsimd", lambda e: e.dma_start(out=g_dram.ap(), in_=gT[:, :]),
                  deps=[t_ev2, t_ev2b], dma=True)
    p0_done = [t_ev, t_ev2, t_g]

    ar.reset(p0m)
    wqk_sb = ar.alloc("wqk", [128, 8 * 128], F32)
    wv_sb = ar.alloc("wv", [128, 8 * 512], F32)
    k_fm = ar.alloc("kfm", [128, S], F32)
    q_own = ar.alloc("qown", [128, 2048], F32)
    v_sb = ar.alloc("vsb", [128, NT * 1024], F32)
    xa = [ar.alloc(f"xa{i}", [128, D], F32) for i in range(2)]
    lnfm = ar.alloc("lnfm", [128, 8 * 128], F32)
    bvbc_sb = ar.alloc("bvbc", [128, D], F32)
    qk_sb = ar.alloc("qksb", [128, 128], F32)
    slab = [ar.alloc(f"slab{i}", [128, 128], F32) for i in range(3)]
    attn = [ar.alloc(f"attn{i}", [128, 128], F32) for i in range(2)]
    tmp1 = [ar.alloc(f"tmp1{i}", [128, 128], F32) for i in range(2)]
    x1sb = [ar.alloc(f"x1sb{i}", [128, D], F32) for i in range(2)]

    t_wqk = sch.add("sync", lambda e: e.dma_start(
        out=wqk_sb[:, :].rearrange("p (c m) -> p c m", m=128),
        in_=Wqk_i.ap().rearrange("(c p) m -> p c m", p=128)), deps=p0_done, dma=True)
    t_bv = sch.add("sync", lambda e: e.dma_start(out=bvbc_sb[:, :], in_=bvbc_a.ap()),
                   deps=p0_done, dma=True)

    st_ = dict(xa_rd=[None, None], lnfm_rd=[], ps67_rd=[None], ps2_rd=None,
               ps3_rd=None, qk_rd=None)

    def emit_ln(x_tile, dep_load, out_ap=None, out_dep=None):
        if out_ap is None:
            out_ap = x_tile[:, :]
        if out_dep is None:
            out_dep = []
        elif not isinstance(out_dep, list):
            out_dep = [out_dep]
        t1 = sch.add("vector", lambda e: e.reduce_sum(small[:, 0:1], x_tile[:, :], axis=AX.X),
                     deps=[dep_load])
        t2 = sch.add("vector", lambda e: e.tensor_scalar_mul(small[:, 0:1], small[:, 0:1],
                                                             -1.0 / D), hard=[t1])
        t3 = sch.add("scalar", lambda e: e.activation(scr[:, :], x_tile[:, :], AF.Square,
                                                      bias=small[:, 0:1],
                                                      accum_out=small[:, 1:2]),
                     deps=[t2])
        t4 = sch.add("vector", lambda e: e.tensor_scalar(small[:, 2:3], small[:, 1:2],
                                                         1.0 / D, EPS, ALU.mult, ALU.add),
                     deps=[t3])
        t5 = sch.add("scalar", lambda e: e.activation(small[:, 3:4], small[:, 2:3], AF.Sqrt),
                     deps=[t4])
        t6 = sch.add("vector", lambda e: e.reciprocal(small[:, 4:5], small[:, 3:4]), deps=[t5])
        t7 = sch.add("vector", lambda e: e.tensor_scalar(out_ap, x_tile[:, :],
                                                         small[:, 0:1], small[:, 4:5],
                                                         ALU.add, ALU.mult),
                     deps=out_dep, hard=[t6])
        return t7

    def emit_transpose(ln_tok, x_tile, dst=None, dst_rd=()):
        if dst is None:
            dst = lnfm
            dst_rd = st_["lnfm_rd"]
        tts = []
        for c in range(8):
            ps = PS[6] if c < 4 else PS[7]
            col = (c % 4) * 128
            tts.append(sch.add(
                "tensor",
                (lambda cc, pps, ccol: lambda e: e.transpose(
                    pps[:, ccol:ccol + 128], x_tile[:, cc * 128:(cc + 1) * 128],
                    ident[:, :]))(c, ps, col),
                deps=[ln_tok, t_ident, st_["ps67_rd"][0]]))
        ev0 = sch.add("vector", (lambda dd: lambda e: e.tensor_copy(
            dd[:, 0:512], PS[6][:, :]))(dst), deps=tts[:4] + list(dst_rd))
        ev1 = sch.add("vector", (lambda dd: lambda e: e.tensor_copy(
            dd[:, 512:1024], PS[7][:, :]))(dst), deps=tts[4:])
        st_["ps67_rd"][0] = ev1
        return [ev0, ev1]

    def emit_qk(evs, dst_ap, gain, boff, extra_deps=()):
        qmm = [sch.add(
            "tensor",
            (lambda kk: lambda e: e.matmul(
                PS[3][:, 0:128], wqk_sb[:, kk * 128:(kk + 1) * 128],
                lnfm[:, kk * 128:(kk + 1) * 128],
                start=(kk == 0), stop=(kk == 7)))(kc),
            deps=(evs + [t_wqk] + list(extra_deps) + ([st_["ps3_rd"]] if kc == 0 else [])))
            for kc in range(8)]
        t_qkA = sch.add("scalar", lambda e: e.activation(
            qk_sb[:, :], PS[3][:, 0:128], AF.Silu, bias=bqk_sb[:, :]),
            deps=qmm + [t_bqk, st_["qk_rd"]])
        st_["ps3_rd"] = t_qkA
        t_aff = sch.add("vector", lambda e: e.tensor_scalar(
            dst_ap, qk_sb[:, :], gain, boff, ALU.mult, ALU.add),
            deps=[t_qkA, t_g1, t_b1, t_g0, t_b0])
        st_["qk_rd"] = t_aff
        return qmm, t_aff

    st_dbg = {}
    v_tok = [None] * NT
    k_tok = [None] * NT
    for hv in ((0, 1) if do_A1 else ()):
        t_wv = sch.add("sync", (lambda hh: lambda e: e.dma_start(
            out=wv_sb[:, :].rearrange("p (c m) -> p c m", m=512),
            in_=Wv_a.ap()[:, hh * 512:(hh + 1) * 512].rearrange("(c p) m -> p c m", p=128)))(hv),
            deps=(v_tok[:] if hv else p0_done), dma=True)
        for t in range(NT):
            sl = t % 2
            t_load = sch.add("scalar", (lambda tt, ss: lambda e: e.dma_start(
                out=xa[ss][:, :], in_=x_row.ap()[tt * 128:(tt + 1) * 128, :]))(t, sl),
                deps=[st_["xa_rd"][sl], st_dbg.get("sm")]
                + (p0_done if (hv == 0 and t < 2) else []), dma=True)
            t_ln = emit_ln(xa[sl], t_load)
            if stage == "dbgA1" and hv == 0 and t == 0:
                st_dbg["ln"] = sch.add("gpsimd", lambda e: e.dma_start(
                    out=out_ext.ap()[896:1024, :], in_=xa[0][:, :]),
                    deps=[t_ln], dma=True)
                st_dbg["sm"] = sch.add("gpsimd", lambda e: e.dma_start(
                    out=out_ext.ap()[1024:1152, 0:8], in_=small[:, 0:8]),
                    deps=[t_ln], dma=True)
                t_ln = st_dbg["sm"]
            evs = emit_transpose(t_ln, xa[sl])
            st_["xa_rd"][sl] = evs[1]
            vmm = [sch.add(
                "tensor",
                (lambda kk: lambda e: e.matmul(
                    PS[2][:, :], lnfm[:, kk * 128:(kk + 1) * 128],
                    wv_sb[:, kk * 512:(kk + 1) * 512],
                    start=(kk == 0), stop=(kk == 7)))(kc),
                deps=(evs + [t_wv] + ([st_["ps2_rd"]] if kc == 0 else [])))
                for kc in range(8)]
            t_badd = sch.add("vector", (lambda hh: lambda e: e.tensor_add(
                PS[2][:, :], PS[2][:, :], bvbc_sb[:, hh * 512:(hh + 1) * 512]))(hv),
                deps=[vmm[-1], t_bv])
            t_vev = sch.add("scalar", (lambda tt, hh: lambda e: e.activation(
                v_sb[:, tt * 1024 + hh * 512: tt * 1024 + hh * 512 + 512],
                PS[2][:, :], AF.Silu))(t, hv), deps=[t_badd])
            st_["ps2_rd"] = t_vev
            v_tok[t] = t_vev
            st_["lnfm_rd"] = vmm[:]
            if hv == 0:
                qmm, t_k = emit_qk(evs, k_fm[:, t * 128:(t + 1) * 128], g1_sb[:, :],
                                   b1a_sb[:, :])
                k_tok[t] = t_k
                st_["lnfm_rd"] = vmm + qmm

    q_tok = [None] * NOWN
    for s in range(NOWN if do_A1 else 0):
        sl = s % 2
        t_load = sch.add("scalar", (lambda ss, sl_: lambda e: e.dma_start(
            out=xa[sl_][:, :], in_=x_own.ap()[ss * 128:(ss + 1) * 128, :]))(s, sl),
            deps=[st_["xa_rd"][sl]], dma=True)
        t_ln = emit_ln(xa[sl], t_load)
        evs = emit_transpose(t_ln, xa[sl])
        st_["xa_rd"][sl] = evs[1]
        qmm, t_q = emit_qk(evs, q_own[:, s * 128:(s + 1) * 128], g0_sb[:, :], b0_sb[:, :],
                           extra_deps=st_["lnfm_rd"])
        q_tok[s] = t_q
        st_["lnfm_rd"] = qmm[:]

    x1_toks = []
    tr_ = dict(attn_rd=[None, None], slab_rd=[None, None, None], tmp_rd=[None, None],
               score_ev=[None, None], x1_rd=[None, None], av_done=None)
    for s_ in range(NOWN if do_A2 else 0):
        n_j = 2 * s_ + 2
        av_last = None
        for j in range(n_j):
            par = j % 2
            sl3 = j % 3
            base = 3968 - (2 * s_ - j) * 128
            t_slab = sch.add("sync", (lambda bb, s3: lambda e: e.dma_start(
                out=slab[s3][:, :], in_=AP(g_dram, bb, [[1, 128], [1, 128]])))(base, sl3),
                deps=[t_g, tr_["slab_rd"][sl3]], dma=True)
            deps_sc = [k_tok[j], q_tok[s_], tr_["score_ev"][par]]
            t_scm = sch.add("tensor", (lambda jj, ss2, pp: lambda e: e.matmul(
                PS[4 + pp][:, 0:128], k_fm[:, jj * 128:(jj + 1) * 128],
                q_own[:, ss2 * 128:(ss2 + 1) * 128], start=True, stop=True))(j, s_, par),
                deps=deps_sc)
            t_e1 = sch.add("vector", (lambda pp, s3: lambda e: e.scalar_tensor_tensor(
                tmp1[pp][:, :], PS[4 + pp][:, 0:128], scc[:, 0:1],
                slab[s3][:, 127::-1], ALU.mult, ALU.add))(par, sl3),
                deps=[t_scm, t_slab, t_sc0, tr_["tmp_rd"][par]])
            tr_["score_ev"][par] = t_e1
            tr_["slab_rd"][sl3] = t_e1
            t_e2 = sch.add("scalar", (lambda pp: lambda e: e.activation(
                attn[pp][:, :], tmp1[pp][:, :], AF.Erf))(par),
                deps=[t_e1, tr_["attn_rd"][par]])
            tr_["tmp_rd"][par] = t_e2
            t_e3 = sch.add("vector", (lambda pp: lambda e: e.tensor_scalar(
                attn[pp][:, :], attn[pp][:, :], scc[:, 1:2], scc[:, 1:2],
                ALU.mult, ALU.add))(par), deps=[t_e2, t_sc1])
            av_deps0 = [t_e3, v_tok[j]]
            if s_ == 0 and j == 0:
                av_deps0 += [st_["ps2_rd"], st_["ps3_rd"], v_tok[NT - 1], q_tok[NOWN - 1]]
            if j == 0:
                av_deps0 += [tr_["av_done"]]
            av0 = sch.add("tensor", (lambda jj, pp: lambda e: e.matmul(
                PS[2][:, :], attn[pp][:, :], v_sb[:, jj * 1024:jj * 1024 + 512],
                start=(jj == 0), stop=(jj == n_j - 1)))(j, par), deps=av_deps0)
            av1 = sch.add("tensor", (lambda jj, pp: lambda e: e.matmul(
                PS[3][:, :], attn[pp][:, :], v_sb[:, jj * 1024 + 512:jj * 1024 + 1024],
                start=(jj == 0), stop=(jj == n_j - 1)))(j, par),
                deps=[t_e3, v_tok[j]])
            tr_["attn_rd"][par] = av1
            av_last = av1
        sl = s_ % 2
        t_xl = sch.add("scalar", (lambda ss2, sl_: lambda e: e.dma_start(
            out=xa[sl_][:, :], in_=x_own.ap()[ss2 * 128:(ss2 + 1) * 128, :]))(s_, sl),
            deps=[st_["xa_rd"][sl]], dma=True)
        e0 = sch.add("vector", (lambda sl_: lambda e: e.tensor_add(
            x1sb[sl_][:, 0:512], PS[2][:, :], xa[sl_][:, 0:512]))(sl),
            deps=[av_last, t_xl, tr_["x1_rd"][sl]])
        e1 = sch.add("vector", (lambda sl_: lambda e: e.tensor_add(
            x1sb[sl_][:, 512:1024], PS[3][:, :], xa[sl_][:, 512:1024]))(sl),
            deps=[av_last, t_xl, tr_["x1_rd"][sl]])
        st_["xa_rd"][sl] = e1
        tr_["av_done"] = e1
        dst = out_ext if stage == "x1" else x1_own
        t_st = sch.add("gpsimd", (lambda ss2, sl_, dd: lambda e: e.dma_start(
            out=dd.ap()[ss2 * 128:(ss2 + 1) * 128, :], in_=x1sb[sl_][:, :]))(s_, sl, dst),
            deps=[e0, e1], dma=True)
        tr_["x1_rd"][sl] = t_st
        x1_toks.append(t_st)

    do_F1 = stage in ("x2", "x3", "full", "dbgF", "dbgB", "dbgF2", "dbgF3", "fullnc", "fullb1", "fullb2", "fullr", "fulle1", "fulle2", "fulle4")
    do_EXB = stage in ("x3", "full", "dbgB", "fullnc", "fullb1", "fullb2", "fullr", "fulle1", "fulle2", "fulle4")
    do_F2 = stage in ("full", "dbgF2", "dbgF3", "fullnc", "fullb1", "fullb2", "fullr", "fulle1", "fulle2", "fulle4")

    f_toks = []
    if do_F1:
        gate_i = inp("gate", [D, NEXP])
        w1_i = inp("w1", [NEXP, D, DFF])
        b1r_i = inp("b1r", [128, NEXP * 32])
        w2_i = inp("w2", [NEXP, DFF, D])
        b2bc_i = inp("b2bc", [NEXP, 128, D])
        compact = stage in ("full", "x2", "x3", "dbgB", "fullc")
        if compact:
            w1R_i = inp("w1R", [NEXP, 8, 128, 8, 512])
            w2R_i = inp("w2R", [NEXP, 8, 128, 4, 1024])
            b1f_i = inp("b1f", [NEXP, DFF])
            b2f_i = inp("b2f", [NEXP, D])
            ltri_i = inp("ltri", [128, 128])
            m16_i = inp("m16", [128, 128])
            P8_i = inp("P8", [128, 128])
            iotaF_i = inp("iotaF", [128, 8])
            xcomp2 = nc.dram_tensor("xcomp2", [4096, D], F32, kind="Internal")
            ycomp2 = nc.dram_tensor("ycomp2", [4096, D], F32, kind="Internal")
        xcomp = nc.dram_tensor("xcomp", [NCOMP, D], F32, kind="Internal")
        ycomp = nc.dram_tensor("ycomp", [NCOMP, D], F32, kind="Internal")
        x2_own = nc.dram_tensor("x2_own", [2048, D], F32, kind="Internal")
        x3_own = nc.dram_tensor("x3_own", [2048, D], F32, kind="Internal")
        cc_in = nc.dram_tensor("cc_in", [2048, D], F32, kind="Internal")
        cc_out = nc.dram_tensor("cc_out", [2 * 2048, D], F32, kind="Internal",
                                addr_space="Local")

        a_done = [x1_toks[-1], tr_["attn_rd"][0], tr_["attn_rd"][1],
                  tr_["tmp_rd"][0], tr_["tmp_rd"][1],
                  tr_["score_ev"][0], tr_["score_ev"][1]]

        ar.reset(p0m)
        big1 = ar.alloc("big1", [128, 16384], F32)
        lnsb = [big1[:, c * 1024:(c + 1) * 1024] for c in range(16)]
        h_sb = big1
        oh_all = ar.alloc("ohall", [128, 128], F32)
        w_all = ar.alloc("wall", [128, 16], F32)
        scr8 = ar.alloc("scr8", [128, 8], F32)
        ones1 = ar.alloc("ones1", [1, 128], F32)
        gate_sb = ar.alloc("gatesb", [128, 64], F32)
        xf = [ar.alloc(f"xf{i}", [128, D], F32) for i in range(2)]
        xe_fm = ar.alloc("xefm", [128, 8 * 512], F32)
        flnfm = xe_fm[:, 0:1024]
        w1sb = [ar.alloc(f"w1sb{i}", [128, 8 * 512], F32) for i in range(2)]
        b2bc_sb = ar.alloc("b2bcsb", [128, D], F32)
        if not compact:
            acc = ar.alloc("acc", [128, 16384], F32)
            maskw = ar.alloc("maskw", [128, 128], F32)
            b1r_sb = ar.alloc("b1rsb", [128, NEXP * 32], F32)
            w2sb = [ar.alloc(f"w2sb{i}", [128, 1024], F32) for i in range(2)]
        else:
            w2c = [ar.alloc(f"w2c{i}", [128, 4 * 1024], F32) for i in range(2)]
            b1c = ar.alloc("b1c", [128, DFF], F32)
            ysbc = [ar.alloc(f"ysbc{i}", [128, D], F32) for i in range(2)]
            idxu = ar.alloc("idxu", [128, 128], U32)
            idxf = ar.alloc("idxf", [128, 128], F32)
            seu = ar.alloc("seu", [128, 16], U32)
            pall = ar.alloc("pall", [128, 128], F32)
            tmp128 = ar.alloc("tmp128", [128, 128], F32)
            ptokf = ar.alloc("ptokf", [128, 16], F32)
            ptoku = ar.alloc("ptoku", [128, 16], U32)
            smallr = ar.alloc("smallr", [1, 64], F32)
            colsr = ar.alloc("colsr", [128, 8], F32)
            rows1 = ar.alloc("rows1", [1, 256], F32)
            onesw = ar.alloc("onesw", [1, 256], F32)
            onescol = ar.alloc("onescol", [128, 1], F32)
            ltri_sb = ar.alloc("ltrisb", [128, 128], F32)
            m16_sb = ar.alloc("m16sb", [128, 128], F32)
            P8_sb = ar.alloc("p8sb", [128, 128], F32)
            iotaF_sb = ar.alloc("iotafsb", [128, 8], F32)
        x2sb = [big1[:, 0:1024], big1[:, 1024:2048]]
        ln2sb = [big1[:, 2048:3072], big1[:, 3072:4096]]
        xfe = [big1[:, 4096:5120], big1[:, 5120:6144]]
        ygat2 = [big1[:, 6144:7168], big1[:, 7168:8192]]

        t_gate = sch.add("sync", lambda e: e.dma_start(
            out=gate_sb[:, :].rearrange("p (c m) -> p c m", m=8),
            in_=gate_i.ap().rearrange("(c p) m -> p c m", p=128)),
            deps=a_done, dma=True)
        if not compact:
            t_b1r = sch.add("sync", lambda e: e.dma_start(out=b1r_sb[:, :], in_=b1r_i.ap()),
                            deps=a_done, dma=True)
        t_ones1 = sch.add("vector", lambda e: e.memset(ones1[:, :], 1.0), deps=a_done)
        if compact:
            t_ltri = sch.add("sync", lambda e: e.dma_start(out=ltri_sb[:, :], in_=ltri_i.ap()),
                             deps=a_done, dma=True)
            t_m16 = sch.add("sync", lambda e: e.dma_start(out=m16_sb[:, :], in_=m16_i.ap()),
                            deps=a_done, dma=True)
            t_p8 = sch.add("sync", lambda e: e.dma_start(out=P8_sb[:, :], in_=P8_i.ap()),
                           dma=True)
            t_iof = sch.add("sync", lambda e: e.dma_start(out=iotaF_sb[:, :], in_=iotaF_i.ap()),
                            dma=True)
            t_onw = sch.add("vector", lambda e: e.memset(onesw[:, :], 1.0), deps=a_done)
            t_onc = sch.add("vector", lambda e: e.memset(onescol[:, :], 1.0), deps=a_done)

        def emit_ffn(call, src_dram, store_fn, first_deps, upto="all"):
            fr = dict(route_ev=[None, None], xf_rd=[None, None],
                      oh_toks=[], w_toks=[])
            ln_st = []
            for c in range(16):
                sl = c % 2
                par = c % 2
                t_load = sch.add("scalar", (lambda cc, ss: lambda e: e.dma_start(
                    out=xf[ss][:, :], in_=src_dram.ap()[cc * 128:(cc + 1) * 128, :]))(c, sl),
                    deps=[fr["xf_rd"][sl]] + (first_deps if c < 2 else []), dma=True)
                t_ln = emit_ln(xf[sl], t_load, out_ap=lnsb[c][:, :],
                               out_dep=(first_deps if c == 0 else None))
                evs = emit_transpose(t_ln, lnsb[c], dst=flnfm,
                                     dst_rd=st_.get("flnfm_rd", []))
                fr["xf_rd"][sl] = t_ln
                rmm = [sch.add(
                    "tensor",
                    (lambda kk, pp: lambda e: e.matmul(
                        PS[4 + pp][:, 0:8], flnfm[:, kk * 128:(kk + 1) * 128],
                        gate_sb[:, kk * 8:(kk + 1) * 8],
                        start=(kk == 0), stop=(kk == 7)))(kc, par),
                    deps=evs + [t_gate] + (fr["route_ev"][par] or [] if kc == 0 else []))
                    for kc in range(8)]
                st_["flnfm_rd"] = rmm
                t_mx = sch.add("vector", (lambda pp: lambda e: e.reduce_max(
                    small[:, 5:6], PS[4 + pp][:, 0:8], axis=AX.X))(par), deps=rmm)
                t_oh = sch.add("vector", (lambda cc, pp: lambda e: e.tensor_scalar(
                    oh_all[:, cc * 8:(cc + 1) * 8], PS[4 + pp][:, 0:8], small[:, 5:6],
                    None, ALU.is_ge))(c, par), hard=[t_mx])
                t_ng = sch.add("vector", lambda e: e.tensor_scalar_mul(
                    small[:, 6:7], small[:, 5:6], -1.0), hard=[t_mx])
                t_ex = sch.add("scalar", (lambda pp: lambda e: e.activation(
                    scr8[:, :], PS[4 + pp][:, 0:8], AF.Exp, bias=small[:, 6:7],
                    accum_out=small[:, 7:8]))(par), deps=[t_ng])
                fr["route_ev"][par] = [t_ex, t_oh]
                t_w = sch.add("vector", (lambda cc: lambda e: e.reciprocal(
                    w_all[:, cc:cc + 1], small[:, 7:8]))(c), deps=[t_ex])
                fr["oh_toks"].append(t_oh)
                fr["w_toks"].append(t_w)
                t_st = sch.add("gpsimd", (lambda cc: lambda e: e.dma_start(
                    out=xcomp.ap()[cc * 128:(cc + 1) * 128, :], in_=lnsb[cc][:, :]))(c),
                    deps=[t_ln], dma=True)
                ln_st.append(t_st)

            mw3 = maskw[:, :].rearrange("p (c e) -> p c e", e=8)
            oh3 = oh_all[:, :].rearrange("p (c e) -> p c e", e=8)
            mw_toks = []
            for e8 in range(8):
                mw_toks.append(sch.add("vector", (lambda ee: lambda e: e.tensor_mul(
                    mw3[:, :, ee], oh3[:, :, ee], w_all[:, :]))(e8),
                    hard=fr["oh_toks"] + fr["w_toks"]))

            if upto == "route":
                return ln_st + mw_toks
            ps_y_rd = {0: t_ev, 1: t_ev2}
            ps_h_rd = [None, None]
            prev_hmm = []
            prev_ymm = []
            w1_rd = [None, None]
            w2_rd = [None, None]
            b2_rd = None
            acc_rd = {}
            comb_toks = {}
            for e_ in range({"e1": 1, "e2": 2, "e4": 4}.get(upto, NEXP)):
                t_b2 = sch.add("sync", (lambda ee: lambda e: e.dma_start(
                    out=b2bc_sb[:, :], in_=b2bc_i.ap()[ee]))(e_),
                    deps=[b2_rd] + (first_deps if e_ == 0 else []), dma=True)
                for grp in range(4):
                    xe_evs = []
                    for tc in range(4):
                        sl = tc % 2
                        t_xe = sch.add("scalar", (lambda gg, tt, ss: lambda e: e.dma_start(
                            out=xf[ss][:, :],
                            in_=xcomp.ap()[gg * 512 + tt * 128: gg * 512 + (tt + 1) * 128, :]))(grp, tc, sl),
                            deps=ln_st + [fr["xf_rd"][sl]], dma=True)
                        tts = []
                        for cc in range(8):
                            ps = PS[6] if cc < 4 else PS[7]
                            col = (cc % 4) * 128
                            tts.append(sch.add(
                                "tensor",
                                (lambda c2, pps, ccol2, ss: lambda e: e.transpose(
                                    pps[:, ccol2:ccol2 + 128], xf[ss][:, c2 * 128:(c2 + 1) * 128],
                                    ident[:, :]))(cc, ps, col, sl),
                                deps=[t_xe, st_["ps67_rd"][0]]))
                        dst0 = xe_fm[:, :].rearrange("p (c m) -> p c m", m=512)[:, 0:4, tc * 128:tc * 128 + 128]
                        dst1 = xe_fm[:, :].rearrange("p (c m) -> p c m", m=512)[:, 4:8, tc * 128:tc * 128 + 128]
                        ev0 = sch.add("vector", (lambda dd: lambda e: e.tensor_copy(
                            dd, PS[6][:, :].rearrange("p (c m) -> p c m", m=128)))(dst0),
                            deps=tts[:4] + ([prev_hmm[-1]] if prev_hmm else [])
                            + (st_["flnfm_rd"] if (e_ == 0 and grp == 0 and tc == 0) else []))
                        ev1 = sch.add("vector", (lambda dd: lambda e: e.tensor_copy(
                            dd, PS[7][:, :].rearrange("p (c m) -> p c m", m=128)))(dst1),
                            deps=tts[4:])
                        st_["ps67_rd"][0] = ev1
                        fr["xf_rd"][sl] = ev1
                        xe_evs += [ev0, ev1]
                    h_toks = []
                    hmm_all = []
                    for ffs in range(8):
                        sl2 = ffs % 2
                        t_w1 = sch.add("sync", (lambda ee, ff, ss: lambda e: e.dma_start(
                            out=w1sb[ss][:, :].rearrange("p (c m) -> p c m", m=512),
                            in_=w1_i.ap()[ee, :, ff * 512:(ff + 1) * 512]
                            .rearrange("(c p) m -> p c m", p=128)))(e_, ffs, sl2),
                            deps=[w1_rd[sl2]]
                            + (first_deps if (e_ == 0 and grp == 0 and ffs < 2) else []),
                            dma=True)
                        for fl in range(4):
                            ffc = ffs * 4 + fl
                            pp = ffc % 2
                            hmm = [sch.add(
                                "tensor",
                                (lambda kk, ss, ffl, p2: lambda e: e.matmul(
                                    PS[4 + p2][:, :],
                                    w1sb[ss][:, kk * 512 + ffl * 128: kk * 512 + ffl * 128 + 128],
                                    xe_fm[:, kk * 512:(kk + 1) * 512],
                                    start=(kk == 0), stop=(kk == 7)))(kc, sl2, fl, pp),
                                deps=xe_evs + [t_w1]
                                + (([ps_h_rd[pp], ps_y_rd.get(4 + pp)]
                                    + (fr["route_ev"][pp] or [])) if kc == 0 else []))
                                for kc in range(8)]
                            hmm_all += hmm
                            t_h = sch.add("scalar", (lambda ffc2, p2: lambda e: e.activation(
                                h_sb[:, ffc2 * 512:(ffc2 + 1) * 512], PS[4 + p2][:, :],
                                AF.Gelu,
                                bias=b1r_sb[:, e_ * 32 + ffc2: e_ * 32 + ffc2 + 1]))(ffc, pp),
                                deps=hmm + (ln_st if (e_ == 0 and grp == 0) else [])
                                + ([prev_ymm[-1]] if prev_ymm else []))
                            ps_h_rd[pp] = t_h
                            h_toks.append(t_h)
                        w1_rd[sl2] = hmm_all[-1]
                    prev_hmm = hmm_all
                    ymm_last = []
                    for ffc in range(32):
                        sl2 = ffc % 2
                        t_w2 = sch.add("sync", (lambda ee, ff, ss: lambda e: e.dma_start(
                            out=w2sb[ss][:, :],
                            in_=w2_i.ap()[ee, ff * 128:(ff + 1) * 128, :]))(e_, ffc, sl2),
                            deps=[w2_rd[sl2]]
                            + (first_deps if (e_ == 0 and grp == 0 and ffc < 2) else []),
                            dma=True)
                        ymm = []
                        for tc in range(4):
                            for half in range(2):
                                bk = 2 * tc + half
                                extra = []
                                if ffc == 0:
                                    extra = [ps_y_rd.get(bk)]
                                    if bk in (4, 5):
                                        extra.append(ps_h_rd[bk - 4])
                                ymm.append(sch.add(
                                    "tensor",
                                    (lambda ffc2, tt, hh, ss, b2k: lambda e: e.matmul(
                                        PS[b2k][:, :],
                                        h_sb[:, ffc2 * 512 + tt * 128: ffc2 * 512 + tt * 128 + 128],
                                        w2sb[ss][:, hh * 512:(hh + 1) * 512],
                                        start=(ffc2 == 0), stop=False))(ffc, tc, half, sl2, bk),
                                    deps=[t_w2, h_toks[ffc]] + extra))
                        w2_rd[sl2] = ymm[-1]
                        ymm_last = ymm
                    bias_mm = []
                    for tc in range(4):
                        for half in range(2):
                            bk = 2 * tc + half
                            bias_mm.append(sch.add(
                                "tensor",
                                (lambda hh, b2k: lambda e: e.matmul(
                                    PS[b2k][:, :], ones1[:, :],
                                    b2bc_sb[0:1, hh * 512:(hh + 1) * 512],
                                    start=False, stop=True))(half, bk),
                                deps=[t_b2, t_ones1, ymm_last[-1]]))
                    prev_ymm = bias_mm
                    for tc in range(4):
                        c = grp * 4 + tc
                        for half in range(2):
                            bk = 2 * tc + half
                            mcol = maskw[:, c * 8 + e_: c * 8 + e_ + 1]
                            a_ap = acc[:, c * 1024 + half * 512: c * 1024 + half * 512 + 512]
                            if e_ == 0:
                                t_cb = sch.add("vector",
                                               (lambda aa, mm2, b2k: lambda e: e.tensor_scalar(
                                                   aa, PS[b2k][:, :], mm2, None, ALU.mult))(a_ap, mcol, bk),
                                               deps=[bias_mm[bk]] + mw_toks)
                            else:
                                t_cb = sch.add("vector",
                                               (lambda aa, mm2, b2k: lambda e: e.scalar_tensor_tensor(
                                                   aa, PS[b2k][:, :], mm2, aa,
                                                   ALU.mult, ALU.add))(a_ap, mcol, bk),
                                               deps=[bias_mm[bk], comb_toks[(c, half)]])
                            comb_toks[(c, half)] = t_cb
                            ps_y_rd[bk] = t_cb
                            if bk >= 6:
                                st_["ps67_rd"][0] = t_cb
                b2_rd = t_b2

            if upto in ("e1", "e2", "e4"):
                return [comb_toks[(c, h)] for c in range(16) for h in range(2)]
            out_toks = []
            st_out_rd = [[], []]
            for c in range(16):
                sl = c % 2
                t_x2l = sch.add("scalar", (lambda cc, ss: lambda e: e.dma_start(
                    out=xfe[ss], in_=src_dram.ap()[cc * 128:(cc + 1) * 128, :]))(c, sl),
                    deps=[comb_toks[(c, 0)], comb_toks[(c, 1)],
                          prev_ymm[-1]] + st_out_rd[sl], dma=True)
                t_res = sch.add("vector", (lambda cc, ss: lambda e: e.tensor_add(
                    x2sb[ss], acc[:, cc * 1024:(cc + 1) * 1024], xfe[ss]))(c, sl),
                    deps=[t_x2l, comb_toks[(c, 0)], comb_toks[(c, 1)]])
                toks = store_fn(c, sl, t_res)
                st_out_rd[sl] = list(toks) + [t_res]
                out_toks += toks
            return out_toks

        def emit_ffn_c2(call, src_dram, store_fn, first_deps):
            fr = dict(route_ev=[None, None], xf_rd=[None, None])
            ln_toks = []
            oh_toks = []
            w_toks = []
            for c in range(16):
                sl = c % 2
                par = c % 2
                t_load = sch.add("scalar", (lambda cc, ss: lambda e: e.dma_start(
                    out=xf[ss][:, :], in_=src_dram.ap()[cc * 128:(cc + 1) * 128, :]))(c, sl),
                    deps=[fr["xf_rd"][sl]] + (first_deps if c < 2 else []), dma=True)
                t_ln = emit_ln(xf[sl], t_load, out_ap=lnsb[c][:, :],
                               out_dep=(first_deps if c == 0 else None))
                evs = emit_transpose(t_ln, lnsb[c], dst=flnfm,
                                     dst_rd=st_.get("flnfm_rd", []))
                fr["xf_rd"][sl] = t_ln
                rmm = [sch.add(
                    "tensor",
                    (lambda kk, pp: lambda e: e.matmul(
                        PS[4 + pp][:, 0:8], flnfm[:, kk * 128:(kk + 1) * 128],
                        gate_sb[:, kk * 8:(kk + 1) * 8],
                        start=(kk == 0), stop=(kk == 7)))(kc, par),
                    deps=evs + [t_gate] + (fr["route_ev"][par] or [] if kc == 0 else []))
                    for kc in range(8)]
                st_["flnfm_rd"] = rmm
                t_mx = sch.add("vector", (lambda pp: lambda e: e.reduce_max(
                    small[:, 5:6], PS[4 + pp][:, 0:8], axis=AX.X))(par), deps=rmm)
                t_oh = sch.add("vector", (lambda cc, pp: lambda e: e.tensor_scalar(
                    oh_all[:, cc * 8:(cc + 1) * 8], PS[4 + pp][:, 0:8], small[:, 5:6],
                    None, ALU.is_ge))(c, par), hard=[t_mx])
                t_ng = sch.add("vector", lambda e: e.tensor_scalar_mul(
                    small[:, 6:7], small[:, 5:6], -1.0), hard=[t_mx])
                t_ex = sch.add("scalar", (lambda pp: lambda e: e.activation(
                    scr8[:, :], PS[4 + pp][:, 0:8], AF.Exp, bias=small[:, 6:7],
                    accum_out=small[:, 7:8]))(par), deps=[t_ng])
                fr["route_ev"][par] = [t_ex, t_oh]
                t_w = sch.add("vector", (lambda cc: lambda e: e.reciprocal(
                    w_all[:, cc:cc + 1], small[:, 7:8]))(c), deps=[t_ex])
                oh_toks.append(t_oh)
                w_toks.append(t_w)
                ln_toks.append(t_ln)

            fd = first_deps
            i1 = sch.add("tensor", lambda e: e.matmul(
                PS[0][:, 0:128], ltri_sb[:, :], oh_all[:, :], start=True, stop=True),
                deps=oh_toks + [t_ltri] + fd)
            i2 = sch.add("tensor", lambda e: e.matmul(
                PS[1][0:1, 0:128], onescol[:, :], oh_all[:, :], start=True, stop=True),
                deps=oh_toks + [t_onc] + fd)
            e2a = sch.add("vector", lambda e: e.tensor_copy(rows1[:, 0:128], PS[1][0:1, 0:128]),
                          deps=[i2] + fd)
            i3 = sch.add("tensor", lambda e: e.transpose(
                PS[1][:, 0:1], rows1[:, 0:128], ident[0:1, 0:1]), deps=[e2a])
            e3a = sch.add("vector", lambda e: e.tensor_copy(colsr[:, 0:1], PS[1][:, 0:1]),
                          deps=[i3])
            i4 = sch.add("tensor", lambda e: e.matmul(
                PS[1][:, 1:2], m16_sb[:, :], colsr[:, 0:1], start=True, stop=True),
                deps=[e3a, t_m16])
            e4a = sch.add("vector", lambda e: e.tensor_copy(colsr[:, 1:2], PS[1][:, 1:2]),
                          deps=[i4])
            i5 = sch.add("tensor", lambda e: e.transpose(
                PS[1][0:1, 2:130], colsr[:, 1:2], ident[:, :]), deps=[e4a, t_ident])
            e5a = sch.add("vector", lambda e: e.tensor_copy(rows1[:, 128:256], PS[1][0:1, 2:130]),
                          deps=[i5])
            ib = sch.add("tensor", lambda e: e.matmul(
                PS[2][:, 0:128], ones1[:, :], rows1[:, 128:256], start=True, stop=True),
                deps=[e5a, t_ones1] + fd)
            e6 = sch.add("vector", lambda e: e.reduce_sum(
                rows1[:, 0:8].rearrange("p (a b) -> p a b", b=1),
                rows1[:, 0:128].rearrange("p (c e) -> p e c", e=8), axis=AX.X),
                hard=[e2a])
            e6b = sch.add("vector", lambda e: e.tensor_copy(smallr[:, 0:8], rows1[:, 0:8]),
                          hard=[e6])
            pcs = sch.add("vector", lambda e: e.memset(smallr[:, 8:16], 0.0), deps=fd)
            prev = pcs
            for k in range(8):
                c1 = sch.add("vector", (lambda kk: lambda e: e.tensor_scalar(
                    smallr[:, 16:24], smallr[:, 0:8], 256.0 * kk, None, ALU.is_gt))(k),
                    hard=[e6b, prev])
                prev = sch.add("vector", lambda e: e.tensor_add(
                    smallr[:, 8:16], smallr[:, 8:16], smallr[:, 16:24]), hard=[c1])
            i6 = sch.add("tensor", lambda e: e.transpose(
                PS[1][0:8, 130:131], smallr[:, 8:16], ident[0:1, 0:1]), deps=[prev])
            e7 = sch.add("vector", lambda e: e.tensor_copy(colsr[0:8, 2:3], PS[1][0:8, 130:131]),
                         deps=[i6])
            i7 = sch.add("tensor", lambda e: e.matmul(
                PS[1][0:8, 131:132], ltri_sb[0:8, 0:8], colsr[0:8, 2:3], start=True, stop=True),
                deps=[e7, t_ltri])
            e8 = sch.add("vector", lambda e: e.tensor_scalar_mul(
                colsr[0:8, 3:4], PS[1][0:8, 131:132], 256.0), deps=[i7])
            i8 = sch.add("tensor", lambda e: e.matmul(
                PS[1][:, 132:133], P8_sb[0:8, :], colsr[0:8, 3:4], start=True, stop=True),
                deps=[e8, t_p8])
            e9 = sch.add("vector", lambda e: e.tensor_copy(colsr[:, 4:5], PS[1][:, 132:133]),
                         deps=[i8])
            i9 = sch.add("tensor", lambda e: e.transpose(
                PS[1][0:1, 133:261], colsr[:, 4:5], ident[:, :]), deps=[e9, t_ident])
            e10 = sch.add("vector", lambda e: e.tensor_copy(rows1[:, 0:128], PS[1][0:1, 133:261]),
                          deps=[i9], hard=[e6])
            i10 = sch.add("tensor", lambda e: e.matmul(
                PS[3][:, 0:128], ones1[:, :], rows1[:, 0:128], start=True, stop=True),
                deps=[e10, t_ones1] + fd)
            pa = sch.add("vector", lambda e: e.tensor_copy(pall[:, :], PS[0][:, 0:128]),
                         deps=[i1] + fd)
            pb = sch.add("vector", lambda e: e.tensor_add(pall[:, :], pall[:, :], PS[2][:, 0:128]),
                         deps=[ib], hard=[pa])
            pc2 = sch.add("vector", lambda e: e.tensor_add(pall[:, :], pall[:, :], PS[3][:, 0:128]),
                          deps=[i10], hard=[pb])
            pm = sch.add("vector", lambda e: e.tensor_mul(tmp128[:, :], pall[:, :], oh_all[:, :]),
                         hard=[pc2] + oh_toks)
            pr = sch.add("vector", lambda e: e.reduce_sum(
                ptokf[:, :].rearrange("p (a b) -> p a b", b=1),
                tmp128[:, :].rearrange("p (a b) -> p a b", b=8), axis=AX.X), hard=[pm])
            pcl = sch.add("vector", lambda e: e.tensor_scalar_min(
                ptokf[:, :], ptokf[:, :], 4095.0), hard=[pr])
            pu = sch.add("vector", lambda e: e.tensor_copy(ptoku[:, :], ptokf[:, :]),
                         hard=[pcl])
            i11 = sch.add("tensor", lambda e: e.transpose(
                PS[1][0:1, 261:269], colsr[0:8, 3:4], ident[0:8, 0:8]), deps=[e8, t_ident])
            e11 = sch.add("vector", lambda e: e.tensor_copy(smallr[:, 24:32], PS[1][0:1, 261:269]),
                          deps=[i11])
            prev3 = e11
            for s_ in range(16):
                c2 = sch.add("vector", (lambda s2: lambda e: e.tensor_scalar(
                    smallr[:, 40:48], smallr[:, 24:32], 256.0 * s2, None, ALU.is_le))(s_),
                    hard=[prev3])
                prev3 = sch.add("vector", (lambda s2: lambda e: e.reduce_sum(
                    smallr[:, 48 + s2:49 + s2], smallr[:, 40:48], axis=AX.X))(s_),
                    hard=[c2])
            se_fin = sch.add("vector", lambda e: e.tensor_scalar(
                smallr[:, 48:64], smallr[:, 48:64], -1.0, None, ALU.add), hard=[prev3])
            i12 = sch.add("tensor", lambda e: e.matmul(
                PS[1][:, 269:285], ones1[:, :], smallr[:, 48:64], start=True, stop=True),
                deps=[se_fin])
            idx_t = []
            for ffs in range(8):
                tix = sch.add("vector", (lambda ff: lambda e: e.tensor_scalar(
                    idxf[:, :].rearrange("p (s f) -> p s f", f=8)[:, :, ff],
                    PS[1][:, 269:285], 1024.0, iotaF_sb[:, ff:ff + 1],
                    ALU.mult, ALU.add))(ffs), deps=[i12, t_iof] + fd)
                idx_t.append(tix)
            iu = sch.add("vector", lambda e: e.tensor_copy(idxu[:, :], idxf[:, :]),
                         hard=idx_t)
            su = sch.add("vector", lambda e: e.tensor_copy(seu[:, :], PS[1][:, 269:285]),
                         deps=[i12], hard=[iu])

            sc_toks = []
            for c in range(16):
                sc_toks.append(sch.add("gpsimd", (lambda cc: lambda e: e.indirect_dma_start(
                    out=xcomp2.ap(),
                    out_offset=IndirectOffsetOnAxis(ptoku[:, cc:cc + 1], 0),
                    in_=lnsb[cc][:, :], in_offset=None))(c),
                    deps=[pu, ln_toks[c]], dma=True))

            y_toks = []
            h_rd = list(sc_toks)
            prev_hmm = []
            prev_ymm = []
            w1_rd = [None, None]
            w2_rd = [None, None]
            b1_rd = None
            b2_rd = None
            ps_h_rd = [None, None]
            ps_y_rd = {0: pa, 1: su, 2: pb, 3: pc2}
            for s_ in range(16):
                xe_evs = []
                for tc in range(2):
                    sl = tc % 2
                    t_xe = sch.add("scalar", (lambda s2, tt, ss: lambda e: e.dma_start(
                        out=xf[ss][:, :],
                        in_=xcomp2.ap()[s2 * 256 + tt * 128: s2 * 256 + (tt + 1) * 128, :]))(s_, tc, sl),
                        deps=sc_toks + [fr["xf_rd"][sl]], dma=True)
                    tts = []
                    for cc in range(8):
                        ps = PS[6] if cc < 4 else PS[7]
                        col = (cc % 4) * 128
                        tts.append(sch.add(
                            "tensor",
                            (lambda c2, pps, ccol2, ss: lambda e: e.transpose(
                                pps[:, ccol2:ccol2 + 128], xf[ss][:, c2 * 128:(c2 + 1) * 128],
                                ident[:, :]))(cc, ps, col, sl),
                            deps=[t_xe, st_["ps67_rd"][0]]))
                    dst0 = xe_fm[:, :].rearrange("p (c m) -> p c m", m=512)[:, 0:4, tc * 128:tc * 128 + 128] if False else \
                        xe_fm[:, 0:2048].rearrange("p (c m) -> p c m", m=256)[:, 0:4, tc * 128:tc * 128 + 128]
                    dst1 = xe_fm[:, 0:2048].rearrange("p (c m) -> p c m", m=256)[:, 4:8, tc * 128:tc * 128 + 128]
                    ev0 = sch.add("vector", (lambda dd: lambda e: e.tensor_copy(
                        dd, PS[6][:, :].rearrange("p (c m) -> p c m", m=128)))(dst0),
                        deps=tts[:4] + ([prev_hmm[-1]] if prev_hmm else [])
                        + (st_["flnfm_rd"] if s_ == 0 and tc == 0 else []))
                    ev1 = sch.add("vector", (lambda dd: lambda e: e.tensor_copy(
                        dd, PS[7][:, :].rearrange("p (c m) -> p c m", m=128)))(dst1),
                        deps=tts[4:])
                    st_["ps67_rd"][0] = ev1
                    fr["xf_rd"][sl] = ev1
                    xe_evs += [ev0, ev1]
                t_b1g = sch.add("gpsimd", (lambda s2: lambda e: e.indirect_dma_start(
                    out=b1c[:, :], out_offset=None, in_=b1f_i.ap(),
                    in_offset=IndirectOffsetOnAxis(seu[:, s2:s2 + 1], 0)))(s_),
                    deps=[su, b1_rd], dma=True)
                t_b2g = sch.add("gpsimd", (lambda s2: lambda e: e.indirect_dma_start(
                    out=b2bc_sb[:, :], out_offset=None, in_=b2f_i.ap(),
                    in_offset=IndirectOffsetOnAxis(seu[:, s2:s2 + 1], 0)))(s_),
                    deps=[su, b2_rd], dma=True)
                h_toks = []
                hmm_all = []
                for ffs in range(8):
                    sl2 = ffs % 2
                    t_w1 = sch.add("gpsimd", (lambda s2, ff, ss: lambda e: e.indirect_dma_start(
                        out=w1sb[ss][:, :],
                        out_offset=None,
                        in_=w1R_i.ap().rearrange("e f p c m -> (e f p) (c m)"),
                        in_offset=IndirectOffsetOnAxis(
                            idxu[:, s2 * 8 + ff:s2 * 8 + ff + 1], 0)))(s_, ffs, sl2),
                        deps=[iu, w1_rd[sl2]], dma=True)
                    for fl in range(4):
                        ffc = ffs * 4 + fl
                        pp = ffc % 2
                        hmm = [sch.add(
                            "tensor",
                            (lambda kk, ss, ffl, p2: lambda e: e.matmul(
                                PS[4 + p2][:, 0:256],
                                w1sb[ss][:, kk * 512 + ffl * 128: kk * 512 + ffl * 128 + 128],
                                xe_fm[:, kk * 256:(kk + 1) * 256],
                                start=(kk == 0), stop=False))(kc, sl2, fl, pp),
                            deps=xe_evs + [t_w1]
                            + (([ps_h_rd[pp]] + (fr["route_ev"][pp] or [])) if kc == 0 else []))
                            for kc in range(8)]
                        bmm = sch.add("tensor", (lambda ffc2, p2: lambda e: e.matmul(
                            PS[4 + p2][:, 0:256],
                            b1c[0:1, ffc2 * 128:(ffc2 + 1) * 128],
                            onesw[:, :], start=False, stop=True))(ffc, pp),
                            deps=hmm + [t_b1g, t_onw])
                        hmm_all += hmm + [bmm]
                        t_h = sch.add("scalar", (lambda ffc2, p2: lambda e: e.activation(
                            h_sb[:, ffc2 * 256:(ffc2 + 1) * 256], PS[4 + p2][:, 0:256],
                            AF.Gelu))(ffc, pp),
                            deps=[bmm] + (h_rd if (s_ == 0 and ffs == 0) else [])
                            + ([prev_ymm[-1]] if (ffs == 0 and prev_ymm) else []))
                        ps_h_rd[pp] = t_h
                        h_toks.append(t_h)
                    w1_rd[sl2] = hmm_all[-1]
                prev_hmm = hmm_all
                ymm_last = []
                for g4 in range(8):
                    sl2 = g4 % 2
                    t_w2 = sch.add("gpsimd", (lambda s2, gg, ss: lambda e: e.indirect_dma_start(
                        out=w2c[ss][:, :],
                        out_offset=None,
                        in_=w2R_i.ap().rearrange("e g p c m -> (e g p) (c m)"),
                        in_offset=IndirectOffsetOnAxis(
                            idxu[:, s2 * 8 + gg:s2 * 8 + gg + 1], 0)))(s_, g4, sl2),
                        deps=[iu, w2_rd[sl2]], dma=True)
                    ymm = []
                    for f4 in range(4):
                        ffc = g4 * 4 + f4
                        for tc in range(2):
                            for half in range(2):
                                bk = 2 * tc + half
                                extra = []
                                if ffc == 0:
                                    extra = [ps_y_rd.get(bk)]
                                ymm.append(sch.add(
                                    "tensor",
                                    (lambda ffc2, f42, tt, hh, ss, b2k: lambda e: e.matmul(
                                        PS[b2k][:, :],
                                        h_sb[:, ffc2 * 256 + tt * 128: ffc2 * 256 + tt * 128 + 128],
                                        w2c[ss][:, f42 * 1024 + hh * 512: f42 * 1024 + (hh + 1) * 512],
                                        start=(ffc2 == 0), stop=(ffc2 == 31)))(ffc, f4, tc, half, sl2, bk),
                                    deps=[t_w2, h_toks[ffc]] + extra))
                    w2_rd[sl2] = ymm[-1]
                    ymm_last = ymm
                prev_ymm = ymm_last
                for tc in range(2):
                    ye0 = sch.add("vector", (lambda tt, ss: lambda e: e.tensor_add(
                        ysbc[ss][:, 0:512], PS[2 * tt][:, :], b2bc_sb[:, 0:512]))(tc, tc % 2),
                        deps=ymm_last + [t_b2g, ps_y_rd.get(("ysb", tc % 2))])
                    ye1 = sch.add("vector", (lambda tt, ss: lambda e: e.tensor_add(
                        ysbc[ss][:, 512:1024], PS[2 * tt + 1][:, :], b2bc_sb[:, 512:1024]))(tc, tc % 2),
                        deps=[ymm_last[-1]])
                    ps_y_rd[2 * tc] = ye0
                    ps_y_rd[2 * tc + 1] = ye1
                    t_yst = sch.add("gpsimd", (lambda s2, tt, ss: lambda e: e.dma_start(
                        out=ycomp2.ap()[s2 * 256 + tt * 128: s2 * 256 + (tt + 1) * 128, :],
                        in_=ysbc[ss][:, :]))(s_, tc, tc % 2), deps=[ye0, ye1], dma=True)
                    ps_y_rd[("ysb", tc % 2)] = t_yst
                    y_toks.append(t_yst)
                b1_rd = hmm_all[-1]
                b2_rd = ye1
                h_rd = []

            out_toks = []
            gat_rd = [None, None]
            st_out_rd = [[], []]
            for c in range(16):
                sl = c % 2
                t_g2 = sch.add("gpsimd", (lambda cc, ss: lambda e: e.indirect_dma_start(
                    out=ygat2[ss], out_offset=None,
                    in_=ycomp2.ap(),
                    in_offset=IndirectOffsetOnAxis(ptoku[:, cc:cc + 1], 0)))(c, sl),
                    deps=y_toks + [gat_rd[sl]], dma=True)
                t_x2l = sch.add("scalar", (lambda cc, ss: lambda e: e.dma_start(
                    out=xfe[ss], in_=src_dram.ap()[cc * 128:(cc + 1) * 128, :]))(c, sl),
                    deps=[fr["xf_rd"][sl], prev_ymm[-1]] + st_out_rd[sl], dma=True)
                t_res = sch.add("vector", (lambda cc, ss: lambda e: e.scalar_tensor_tensor(
                    x2sb[ss], ygat2[ss], w_all[:, cc:cc + 1], xfe[ss],
                    ALU.mult, ALU.add))(c, sl),
                    deps=[t_g2, t_x2l], hard=[w_toks[cc]])
                gat_rd[sl] = t_res
                fr["xf_rd"][sl] = t_res
                toks = store_fn(c, sl, t_res)
                st_out_rd[sl] = list(toks) + [t_res]
                out_toks += toks
            return out_toks

        x2sb_rd = [None, None]

        def store1(c, sl, t_res):
            toks = []
            dst1 = out_ext if stage in ("x2", "dbgF") else x2_own
            t_s1 = sch.add("gpsimd", (lambda cc, ss, dd: lambda e: e.dma_start(
                out=dd.ap()[cc * 128:(cc + 1) * 128, :], in_=x2sb[ss]))(c, sl, dst1),
                deps=[t_res], dma=True)
            toks.append(t_s1)
            if do_EXB:
                t_ln2 = emit_ln(x2sb[sl], t_res, out_ap=ln2sb[sl],
                                out_dep=x2sb_rd[sl])
                t_s2 = sch.add("gpsimd", (lambda cc, ss: lambda e: e.dma_start(
                    out=cc_in.ap()[cc * 128:(cc + 1) * 128, :], in_=ln2sb[ss]))(c, sl),
                    deps=[t_ln2], dma=True)
                x2sb_rd[sl] = t_s2
                toks.append(t_s2)
            return toks

        emit_one_ffn = emit_ffn_c2 if compact else emit_ffn
        f_toks = emit_one_ffn(1, x1_own, store1, first_deps=a_done)

    if do_EXB:
        WqT_i = inp("WqT", [D, D])
        WkT_i = inp("WkT", [D, D])
        WvT_i = inp("WvT", [D, D])
        WoT_i = inp("WoT", [D, D])
        bqr_i = inp("bqr", [128, 8])
        bkr_i = inp("bkr", [128, 8])
        bor_i = inp("bor", [128, 8])
        bvbb_i = inp("bvbb", [128, D])
        outfm_dram = nc.dram_tensor("outfm", [D, 2048], F32, kind="Internal")

        GROUPS = [[0, 1], [2, 3], [4, 5], [6, 7]]
        if stage == "fullnc":
            t_cc0 = sch.add("gpsimd", lambda e: e.dma_start(
                out=cc_out.ap()[0:2048, :], in_=cc_in.ap()), deps=f_toks, dma=True)
            t_cc = sch.add("gpsimd", lambda e: e.dma_start(
                out=cc_out.ap()[2048:4096, :], in_=cc_in.ap()), deps=[t_cc0], dma=True)
        else:
            t_cc = sch.add("gpsimd", lambda e: e.collective_compute(
                "AllGather", ALU.bypass, replica_groups=GROUPS,
                ins=[cc_in.ap()], outs=[cc_out.ap()]),
                deps=f_toks, sem="cc")

        ar.reset(p0m)
        vb_sb = ar.alloc("vbsb", [128, NT * 512], F32)
        k8_sb = ar.alloc("k8sb", [128, 8 * S], FP8)
        q8_sb = ar.alloc("q8sb", [128, 8 * 2048], FP8)
        wvb_sb = ar.alloc("wvbsb", [128, 8 * 512], F32)
        wk_sb = [ar.alloc(f"wksb{i}", [128, 8 * 128], F32) for i in range(2)]
        x2fmb = ar.alloc("x2fmb", [128, 8 * 512], F32)
        xb = [ar.alloc(f"xb{i}", [128, D], F32) for i in range(2)]
        attn4 = ar.alloc("attn4", [128, 4 * 1024], F32)
        otr = [ar.alloc(f"otr{i}", [128, 512], F32) for i in range(2)]
        outsp = [ar.alloc(f"outsp{i}", [128, 512], F32) for i in range(2)]
        thr_sb = ar.alloc("thrsb", [128, 25], F32)
        bvbb_sb = ar.alloc("bvbbsb", [128, D], F32)

        b_first = list(f_toks)
        t_bq = sch.add("sync", lambda e: e.dma_start(out=thr_sb[:, 0:8], in_=bqr_i.ap()),
                       deps=b_first, dma=True)
        t_bk = sch.add("sync", lambda e: e.dma_start(out=thr_sb[:, 8:16], in_=bkr_i.ap()),
                       dma=True)
        t_bo = sch.add("sync", lambda e: e.dma_start(out=thr_sb[:, 16:24], in_=bor_i.ap()),
                       dma=True)
        t_bvb = sch.add("sync", lambda e: e.dma_start(out=bvbb_sb[:, :], in_=bvbb_i.ap()),
                        deps=b_first, dma=True)
        t_thr = sch.add("vector", lambda e: e.tensor_scalar(
            thr_sb[:, 0:24], thr_sb[:, 0:24], -1.0, 1.0, ALU.mult, ALU.add),
            deps=[t_bq, t_bk, t_bo] + b_first)
        t_ths = sch.add("vector", lambda e: e.memset(thr_sb[:, 24:25], 0.125),
                        deps=[t_thr], hard=[t_thr])

        def row_tile_src(t):
            r0 = (t % 2) * 2048 + (t // 2) * 128
            return cc_out.ap()[r0:r0 + 128, :]

        btr = dict(xb_rd=[None, None], x2fmb_rd=[], v_rd={}, k8_rd=None, q8_rd=None,
                   wk_rd=[None, None], att_rd=[None, None], sc_ev=[None, None],
                   sig_ev=[None, None], out_ps_rd=[None] * 4, outsp_rd=[None, None],
                   ofm_rd=[], wo_rd=[None, None], ops_rd=[None, None],
                   spike_tts=None, x2l_rd=[None, None])

        def emit_group_transpose(load_tok, sl, tloc, extra=()):
            tts = []
            for cc in range(8):
                ps = PS[6] if cc < 4 else PS[7]
                col = (cc % 4) * 128
                tts.append(sch.add(
                    "tensor",
                    (lambda c2, pps, ccol2, ss: lambda e: e.transpose(
                        pps[:, ccol2:ccol2 + 128], xb[ss][:, c2 * 128:(c2 + 1) * 128],
                        ident[:, :]))(cc, ps, col, sl),
                    deps=[load_tok, st_["ps67_rd"][0]]))
            dst0 = x2fmb[:, :].rearrange("p (c m) -> p c m", m=512)[:, 0:4, tloc * 128:tloc * 128 + 128]
            dst1 = x2fmb[:, :].rearrange("p (c m) -> p c m", m=512)[:, 4:8, tloc * 128:tloc * 128 + 128]
            ev0 = sch.add("vector", (lambda dd: lambda e: e.tensor_copy(
                dd, PS[6][:, :].rearrange("p (c m) -> p c m", m=128)))(dst0),
                deps=tts[:4] + btr["x2fmb_rd"] + list(extra))
            ev1 = sch.add("vector", (lambda dd: lambda e: e.tensor_copy(
                dd, PS[7][:, :].rearrange("p (c m) -> p c m", m=128)))(dst1),
                deps=tts[4:])
            st_["ps67_rd"][0] = ev1
            btr["xb_rd"][sl] = ev1
            return [ev0, ev1]

        for tg in range(4):
            evs_all = []
            old_rd = btr["x2fmb_rd"]
            for tloc in range(4):
                sl = tloc % 2
                t_l = sch.add("scalar", (lambda gg, tt, ss: lambda e: e.dma_start(
                    out=xb[ss][:, :],
                    in_=cc_in.ap()[gg * 512 + tt * 128: gg * 512 + (tt + 1) * 128, :]))(tg, tloc, sl),
                    deps=[btr["xb_rd"][sl]] + (b_first if (tg == 0 and tloc < 2) else []),
                    dma=True)
                evs_all += emit_group_transpose(t_l, sl, tloc,
                                                extra=(b_first if (tg == 0 and tloc == 0) else ()))
            btr["x2fmb_rd"] = []
            for ko in range(8):
                sl2 = ko % 2
                t_wq = sch.add("sync", (lambda kk, ss: lambda e: e.dma_start(
                    out=wk_sb[ss][:, :].rearrange("p (c m) -> p c m", m=128),
                    in_=WqT_i.ap()[:, kk * 128:(kk + 1) * 128]
                    .rearrange("(c p) m -> p c m", p=128)))(ko, sl2),
                    deps=[btr["wk_rd"][sl2]] + (b_first if (tg == 0 and ko < 2) else []),
                    dma=True)
                par = ko % 2
                qmm = [sch.add(
                    "tensor",
                    (lambda kk, ss, p2: lambda e: e.matmul(
                        PS[4 + p2][:, :], wk_sb[ss][:, kk * 128:(kk + 1) * 128],
                        x2fmb[:, kk * 512:(kk + 1) * 512],
                        start=(kk == 0), stop=(kk == 7)))(kc, sl2, par),
                    deps=evs_all + [t_wq] + ([btr["sc_ev"][par]] if kc == 0 else []))
                    for kc in range(8)]
                btr["x2fmb_rd"] += qmm
                t_sp = sch.add("vector", (lambda kk, gg, p2: lambda e: e.tensor_scalar(
                    q8_sb[:, kk * 2048 + gg * 512: kk * 2048 + (gg + 1) * 512],
                    PS[4 + p2][:, :], thr_sb[:, kk:kk + 1], None, ALU.is_ge))(ko, tg, par),
                    deps=qmm + [t_thr])
                btr["sc_ev"][par] = t_sp
                btr["wk_rd"][sl2] = qmm[-1]
                btr["q8_rd"] = t_sp

        ofm_st_toks = []
        for g in range(2):
            t_wv = sch.add("sync", (lambda gg: lambda e: e.dma_start(
                out=wvb_sb[:, :].rearrange("p (c m) -> p c m", m=512),
                in_=WvT_i.ap()[:, gg * 512:(gg + 1) * 512]
                .rearrange("(c p) m -> p c m", p=128)))(g),
                deps=list(btr["v_rd"].values()) + (b_first if g == 0 else []), dma=True)
            for tg in range(8):
                evs_all = []
                for tloc in range(4):
                    sl = tloc % 2
                    t_l = sch.add("scalar", (lambda t4, ss: lambda e: e.dma_start(
                        out=xb[ss][:, :], in_=row_tile_src(t4)))(tg * 4 + tloc, sl),
                        deps=[btr["xb_rd"][sl], t_cc], dma=True)
                    evs_all += emit_group_transpose(t_l, sl, tloc)
                btr["x2fmb_rd"] = []
                for tloc in range(4):
                    t4 = tg * 4 + tloc
                    par = tloc % 2
                    vmm = [sch.add(
                        "tensor",
                        (lambda kk, tt, p2: lambda e: e.matmul(
                            PS[4 + p2][:, :],
                            x2fmb[:, kk * 512 + tt * 128: kk * 512 + tt * 128 + 128],
                            wvb_sb[:, kk * 512:(kk + 1) * 512],
                            start=(kk == 0), stop=(kk == 7)))(kc, tloc, par),
                        deps=evs_all + [t_wv] + ([btr["sc_ev"][par]] if kc == 0 else []))
                        for kc in range(8)]
                    btr["x2fmb_rd"] += vmm
                    t_v = sch.add("vector", (lambda t5, gg, p2: lambda e: e.tensor_add(
                        vb_sb[:, t5 * 512:(t5 + 1) * 512], PS[4 + p2][:, :],
                        bvbb_sb[:, gg * 512:(gg + 1) * 512]))(t4, g, par),
                        deps=vmm + [t_bvb, btr["v_rd"].get(t4)])
                    btr["sc_ev"][par] = t_v
                    btr["v_rd"][t4] = t_v
                if g == 0:
                    for ko in range(8):
                        sl2 = ko % 2
                        t_wk = sch.add("sync", (lambda kk, ss: lambda e: e.dma_start(
                            out=wk_sb[ss][:, :].rearrange("p (c m) -> p c m", m=128),
                            in_=WkT_i.ap()[:, kk * 128:(kk + 1) * 128]
                            .rearrange("(c p) m -> p c m", p=128)))(ko, sl2),
                            deps=[btr["wk_rd"][sl2]], dma=True)
                        par = ko % 2
                        kmm = [sch.add(
                            "tensor",
                            (lambda kk, ss, p2: lambda e: e.matmul(
                                PS[4 + p2][:, :], wk_sb[ss][:, kk * 128:(kk + 1) * 128],
                                x2fmb[:, kk * 512:(kk + 1) * 512],
                                start=(kk == 0), stop=(kk == 7)))(kc, sl2, par),
                            deps=evs_all + [t_wk] + ([btr["sc_ev"][par]] if kc == 0 else []))
                            for kc in range(8)]
                        btr["x2fmb_rd"] += kmm
                        t_sp = sch.add("vector", (lambda kk, gg, p2: lambda e: e.tensor_scalar(
                            k8_sb[:, kk * S + gg * 512: kk * S + (gg + 1) * 512],
                            PS[4 + p2][:, :], thr_sb[:, 8 + kk:9 + kk], None,
                            ALU.is_ge))(ko, tg, par),
                            deps=kmm + [t_thr])
                        btr["sc_ev"][par] = t_sp
                        btr["wk_rd"][sl2] = kmm[-1]
                        btr["k8_rd"] = t_sp
            if stage == "dbgB" and g == 0:
                t_dv = sch.add("gpsimd", lambda e: e.dma_start(
                    out=out_ext.ap()[256:384, 0:512], in_=vb_sb[:, 0:512]),
                    deps=[btr["v_rd"][0]], dma=True)
                btr["v_rd"][0] = t_dv
            if stage == "fullb1":
                continue
            if "att4_rd" not in btr:
                btr["att4_rd"] = [None] * 4
                btr["psw_rd"] = [None, None]
                btr["outq_rd"] = [None] * 4
                btr["otr_rd"] = [None, None]
            for sg in range(4):
                av_lasts = [None] * 4
                for j in range(NT):
                    sig_toks = []
                    for hp_loc in range(4):
                        hp = 4 * g + hp_loc
                        par = hp_loc % 2
                        first_sc = (sg == 0 and j == 0 and hp_loc < 2)
                        smm = []
                        for a in range(2):
                            bp = a * 64
                            smm.append(sch.add(
                                "tensor",
                                (lambda hh, jj, qq, aa, bpp, p2: lambda e: e.matmul(
                                    PSW[p2][:, aa * 512:(aa + 1) * 512],
                                    k8_sb[bpp:bpp + 64,
                                          hh * S + jj * 128: hh * S + (jj + 1) * 128],
                                    q8_sb[bpp:bpp + 64,
                                          hh * 2048 + qq * 512: hh * 2048 + (qq + 1) * 512],
                                    start=True, stop=True,
                                    tile_position=(bpp, 0)))(hp, j, sg, a, bp, par),
                                deps=[btr["k8_rd"], btr["q8_rd"], btr["psw_rd"][par]]
                                + ([btr["sc_ev"][0], btr["sc_ev"][1]] if first_sc else [])))
                        t_sig = sch.add("scalar", (lambda hl, p2: lambda e: e.activation(
                            attn4[:, hl * 1024:(hl + 1) * 1024], PSW[p2][:, :], AF.Sigmoid,
                            scale=thr_sb[:, 24:25]))(hp_loc, par),
                            deps=smm + [t_ths, btr["att4_rd"][hp_loc]])
                        btr["psw_rd"][par] = t_sig
                        sig_toks.append(t_sig)
                        if stage == "dbgB" and g == 0 and sg == 0 and hp_loc == 0 and j == 0:
                            t_da = sch.add("gpsimd", lambda e: e.dma_start(
                                out=out_ext.ap()[768:896, 0:1024], in_=attn4[:, 0:1024]),
                                deps=[t_sig], dma=True)
                            btr["att4_rd"][0] = t_da
                    for qs_ in range(4):
                        for hd in range(8):
                            hl = hd // 2
                            a = hd % 2
                            av_deps = [sig_toks[hl], btr["v_rd"][j]]
                            if j == 0 and hd == 0:
                                av_deps.append(btr["outq_rd"][qs_])
                            t_av = sch.add(
                                "tensor",
                                (lambda jj, qv, hd2, hl2, aa: lambda e: e.matmul(
                                    PS[qv][:, hd2 * 64:(hd2 + 1) * 64],
                                    attn4[:, hl2 * 1024 + aa * 512 + qv * 128:
                                          hl2 * 1024 + aa * 512 + qv * 128 + 128],
                                    vb_sb[:, jj * 512 + hd2 * 64: jj * 512 + hd2 * 64 + 64],
                                    start=(jj == 0 and hd2 == 0),
                                    stop=(jj == NT - 1 and hd2 == 7)))(j, qs_, hd, hl, a),
                                deps=av_deps)
                            btr["att4_rd"][hl] = t_av
                            av_lasts[qs_] = t_av
                for qs_ in range(4):
                    osl = qs_ % 2
                    t_oe = sch.add("vector", (lambda ss, qv: lambda e: e.tensor_copy(
                        outsp[ss][:, :], PS[qv][:, :]))(osl, qs_),
                        deps=[av_lasts[3], btr["outsp_rd"][osl]])
                    btr["outq_rd"][qs_] = t_oe
                    if stage == "dbgB" and g == 0 and sg == 0 and qs_ == 0:
                        t_do = sch.add("gpsimd", lambda e: e.dma_start(
                            out=out_ext.ap()[896:1024, 0:512], in_=outsp[0][:, :]),
                            deps=[t_oe], dma=True)
                        btr["outsp_rd"][0] = t_do
                    tts2 = []
                    for b in range(4):
                        ps = PS[6] if b < 2 else PS[7]
                        col = (b % 2) * 128
                        tts2.append(sch.add(
                            "tensor",
                            (lambda bb, pps, cc2, ss: lambda e: e.transpose(
                                pps[:, cc2:cc2 + 128],
                                outsp[ss][:, bb * 128:(bb + 1) * 128],
                                ident[:, :]))(b, ps, col, osl),
                            deps=[t_oe, st_["ps67_rd"][0], btr["psw_rd"][1]]))
                    t_tr0 = sch.add("vector", (lambda ss: lambda e: e.tensor_copy(
                        otr[ss][:, 0:256], PS[6][:, 0:256]))(osl),
                        deps=tts2[:2] + [btr["otr_rd"][osl]])
                    t_tr1 = sch.add("vector", (lambda ss: lambda e: e.tensor_copy(
                        otr[ss][:, 256:512], PS[7][:, 0:256]))(osl), deps=tts2[2:])
                    st_["ps67_rd"][0] = t_tr1
                    btr["psw_rd"][1] = t_tr1
                    btr["outsp_rd"][osl] = t_tr1
                    st_toks2 = []
                    for b in range(4):
                        st_toks2.append(sch.add("gpsimd",
                                                (lambda gg, qq, qv, bb, ss: lambda e: e.dma_start(
                            out=outfm_dram.ap()[gg * 512 + bb * 128: gg * 512 + (bb + 1) * 128,
                                                qq * 512 + qv * 128: qq * 512 + qv * 128 + 128],
                            in_=otr[ss][:, bb * 128:(bb + 1) * 128]))(g, sg, qs_, b, osl),
                            deps=[t_tr0, t_tr1], dma=True))
                    btr["otr_rd"][osl] = st_toks2[-1]
                    ofm_st_toks += st_toks2
            btr["sc_ev"][0] = btr["psw_rd"][0]
            btr["sc_ev"][1] = btr["psw_rd"][1]

        if stage == "fullb1":
            b_toks = [btr["k8_rd"], btr["q8_rd"]] + list(btr["v_rd"].values())
        if stage == "fullb2":
            b_toks = list(ofm_st_toks)
        ar.reset(p0m)
        skip_b2 = stage in ("fullb1", "fullb2")
        ofm_sb = ar.alloc("ofmsb", [128, 8 * 512], F32)
        wo_sb = [ar.alloc(f"wosb{i}", [128, 8 * 128], F32) for i in range(2)]
        spikefm = ar.alloc("spikefm", [128, 8 * 512], F32)
        x3sb = [ar.alloc(f"x3sb{i}", [128, D], F32) for i in range(2)]
        x2l = [ar.alloc(f"x2l{i}", [128, D], F32) for i in range(2)]
        if not skip_b2:
            b_toks = []
        x3_store_rd = [None, None]
        for sg in range(4 if not skip_b2 else 0):
            ofm_lds = []
            for hh in range(8):
                ofm_lds.append(sch.add("sync", (lambda h2, qq: lambda e: e.dma_start(
                    out=ofm_sb[:, h2 * 512:(h2 + 1) * 512],
                    in_=outfm_dram.ap()[h2 * 128:(h2 + 1) * 128,
                                        qq * 512:(qq + 1) * 512]))(hh, sg),
                    deps=ofm_st_toks + btr["ofm_rd"] + (b_first if sg == 0 else []),
                    dma=True))
            btr["ofm_rd"] = []
            sp_toks = []
            for oc in range(8):
                sl2 = oc % 2
                t_wo = sch.add("sync", (lambda kk, ss: lambda e: e.dma_start(
                    out=wo_sb[ss][:, :].rearrange("p (c m) -> p c m", m=128),
                    in_=WoT_i.ap()[:, kk * 128:(kk + 1) * 128]
                    .rearrange("(c p) m -> p c m", p=128)))(oc, sl2),
                    deps=[btr["wo_rd"][sl2]]
                    + (ofm_st_toks if (sg == 0 and oc < 2) else []), dma=True)
                par = oc % 2
                first_o = (sg == 0 and oc < 2)
                omm = [sch.add(
                    "tensor",
                    (lambda kk, ss, p2: lambda e: e.matmul(
                        PS[p2][:, :], wo_sb[ss][:, kk * 128:(kk + 1) * 128],
                        ofm_sb[:, kk * 512:(kk + 1) * 512],
                        start=(kk == 0), stop=(kk == 7)))(kc, sl2, par),
                    deps=ofm_lds + [t_wo]
                    + (([btr["ops_rd"][par]] + (btr["out_ps_rd"] if first_o else []))
                       if kc == 0 else []))
                    for kc in range(8)]
                btr["ofm_rd"] += omm
                btr["wo_rd"][sl2] = omm[-1]
                t_spk = sch.add("vector", (lambda occ, p2: lambda e: e.tensor_scalar(
                    spikefm[:, occ * 512:(occ + 1) * 512], PS[p2][:, :],
                    thr_sb[:, 16 + occ:17 + occ], None, ALU.is_ge))(oc, par),
                    deps=omm + [t_thr, btr["spike_tts"]])
                btr["ops_rd"][par] = t_spk
                sp_toks.append(t_spk)
            last_tts = None
            for tc in range(4):
                sl = tc % 2
                c = sg * 4 + tc
                t_x2 = sch.add("scalar", (lambda cc, ss: lambda e: e.dma_start(
                    out=x2l[ss][:, :], in_=x2_own.ap()[cc * 128:(cc + 1) * 128, :]))(c, sl),
                    deps=[btr["x2l_rd"][sl]]
                    + (ofm_st_toks if (sg == 0 and tc < 2) else []), dma=True)
                tts = []
                for occ in range(8):
                    ps = PS[6] if occ < 4 else PS[7]
                    col = (occ % 4) * 128
                    tts.append(sch.add(
                        "tensor",
                        (lambda o2, pps, ccol2, t2: lambda e: e.transpose(
                            pps[:, ccol2:ccol2 + 128],
                            spikefm[:, o2 * 512 + t2 * 128: o2 * 512 + t2 * 128 + 128],
                            ident[:, :]))(occ, ps, col, tc),
                        deps=sp_toks + [st_["ps67_rd"][0]]))
                last_tts = tts[-1]
                e0 = sch.add("vector", (lambda ss: lambda e: e.tensor_copy(
                    x3sb[ss][:, 0:512], PS[6][:, :]))(sl),
                    deps=tts[:4] + [x3_store_rd[sl]])
                e1 = sch.add("vector", (lambda ss: lambda e: e.tensor_copy(
                    x3sb[ss][:, 512:1024], PS[7][:, :]))(sl), deps=tts[4:])
                st_["ps67_rd"][0] = e1
                e2 = sch.add("vector", (lambda ss: lambda e: e.tensor_add(
                    x3sb[ss][:, :], x3sb[ss][:, :], x2l[ss][:, :]))(sl),
                    deps=[e0, e1, t_x2])
                btr["x2l_rd"][sl] = e2
                dstB = out_ext if stage == "x3" else x3_own
                t_st3 = sch.add("gpsimd", (lambda cc, ss, dd: lambda e: e.dma_start(
                    out=dd.ap()[cc * 128:(cc + 1) * 128, :], in_=x3sb[ss][:, :]))(c, sl, dstB),
                    deps=[e2], dma=True)
                x3_store_rd[sl] = t_st3
                b_toks.append(t_st3)
            btr["spike_tts"] = last_tts
        if stage == "dbgB":
            def dumpb(row0, src_ap, deps, cols):
                b_toks.append(sch.add("gpsimd", (lambda r0, sap, cc2: lambda e: e.dma_start(
                    out=out_ext.ap()[r0:r0 + sap.shape[0], 0:cc2], in_=sap))(row0, src_ap, cols),
                    deps=deps, dma=True))
            dumpb(0, q8_sb[:, 0:1024], [btr["q8_rd"]], 1024)
            dumpb(128, k8_sb[:, 0:1024], [btr["k8_rd"]], 1024)
            dumpb(256, vb_sb[:, 0:512], [btr["v_rd"][0]], 512)
            b_toks.append(sch.add("gpsimd", lambda e: e.dma_start(
                out=out_ext.ap()[384:512, 0:1024],
                in_=outfm_dram.ap()[0:128, 0:1024]), deps=ofm_st_toks, dma=True))
            dumpb(512, spikefm[:, 0:1024], sp_toks, 1024)
            b_toks.append(sch.add("gpsimd", lambda e: e.dma_start(
                out=out_ext.ap()[640:768, 0:1024],
                in_=cc_out.ap()[2048:2176, 0:1024]), deps=[t_cc], dma=True))
            b_toks.append(sch.add("gpsimd", lambda e: e.dma_start(
                out=out_ext.ap()[1024:1152, 0:1024],
                in_=outfm_dram.ap()[128:256, 0:1024]), deps=ofm_st_toks, dma=True))
            b_toks.append(sch.add("gpsimd", lambda e: e.dma_start(
                out=out_ext.ap()[1152:1280, 0:1024],
                in_=outfm_dram.ap()[512:640, 0:1024]), deps=ofm_st_toks, dma=True))
        f_toks = b_toks

    if do_F2:
        def store2(c, sl, t_res):
            t_s = sch.add("gpsimd", (lambda cc, ss: lambda e: e.dma_start(
                out=out_ext.ap()[cc * 128:(cc + 1) * 128, :], in_=x2sb[ss]))(c, sl),
                deps=[t_res], dma=True)
            return [t_s]

        src2 = x3_own if do_EXB else x2_own
        if compact:
            f_toks = emit_ffn_c2(2, src2, store2, first_deps=f_toks)
        else:
            upto2 = {"fullr": "route", "fulle1": "e1", "fulle2": "e2", "fulle4": "e4"}.get(stage, "all")
            f_toks = emit_ffn(2, src2, store2, first_deps=f_toks, upto=upto2)
        if stage == "dbgF3":
            f_toks = emit_ffn(3, x2_own, store2, first_deps=f_toks)

    final_deps = list(f_toks) if do_F1 else list(x1_toks)

    if stage in ("dbg1", "dbgG", "dbgA1"):
        def dump(row0, src_ap, deps):
            final_deps.append(sch.add("gpsimd", lambda e: e.dma_start(
                out=out_ext.ap()[row0:row0 + src_ap.shape[0], 0:src_ap.shape[1]],
                in_=src_ap), deps=deps, dma=True))
        dump(768, gT[:, :], [t_ev2])
        if do_A1:
            dump(0, v_sb[:, 0:1024], [v_tok[0]])
            dump(128, v_sb[:, 31 * 1024:32 * 1024], [v_tok[31]])
            dump(256, k_fm[:, 0:1024], k_tok[:8])
            dump(384, k_fm[:, 3072:4096], k_tok[24:32])
            dump(512, q_own[:, 0:1024], q_tok[:8])
            dump(640, q_own[:, 1024:2048], q_tok[8:])
        if do_A2:
            dump(896, attn[1][:, :], [tr_["attn_rd"][1]])
            dump(1024, slab[31 % 3][:, :], [tr_["slab_rd"][31 % 3]])
        if stage == "dbgA1":
            final_deps.extend([st_dbg["ln"], st_dbg["sm"]])

    sch.add("gpsimd", lambda e: e.nop(), deps=final_deps)

    sch.finalize()
    from contextlib import ExitStack
    sem_names = ["sync", "scalar", "vector", "tensor", "gpsimd",
                 "q_sync", "q_scalar", "q_vector", "q_gpsimd", "cc"]
    with ExitStack() as stk:
        sems = {n: stk.enter_context(nc.semaphore("s_" + n)) for n in sem_names}
        with nc.Block() as block:
            for en in ENGS:
                block.__getattribute__(en)(
                    (lambda name: lambda eng: sch.emit_engine(name, eng, sems))(en))
    return nc



_NC_CACHE = {}


def kernel(**inputs):
    stage = STAGE
    f32 = lambda a: np.ascontiguousarray(np.asarray(a), dtype=np.float32)

    x = f32(inputs["L0_attn_a_x"])
    Wqk = f32(inputs["L0_attn_a_Wqk"])
    bqk = f32(inputs["L0_attn_a_bqk"]).reshape(DQK, 1)
    gam = f32(inputs["L0_attn_a_gamma"])
    bet = f32(inputs["L0_attn_a_beta"])
    Wv = f32(inputs["L0_attn_a_Wv"])
    bv = f32(inputs["L0_attn_a_bv"])
    rel = np.zeros((128, 1), dtype=np.float32)
    rel[0:32] = f32(inputs["L0_attn_a_rel_emb"]).reshape(32, 1)

    ident = np.eye(128, dtype=np.float32)
    bvbc = np.ascontiguousarray(np.broadcast_to(bv[None, :], (128, D)), dtype=np.float32)
    Mh = []
    for h in range(2):
        m = np.zeros((128, GLEN), dtype=np.float32)
        m[0:33] = _make_M(h)
        Mh.append(m)

    gate = f32(inputs["L0_ffn_gate_w"])
    w1 = f32(inputs["L0_ffn_w1"])
    b1 = f32(inputs["L0_ffn_b1"])
    w2 = f32(inputs["L0_ffn_w2"])
    b2 = f32(inputs["L0_ffn_b2"])
    b1r = np.concatenate([np.ascontiguousarray(b1[e].reshape(32, 128).T)
                          for e in range(NEXP)], axis=1)
    b2bc = np.ascontiguousarray(np.broadcast_to(b2[:, None, :], (NEXP, 128, D)),
                                dtype=np.float32)

    w1R = np.ascontiguousarray(w1.reshape(8, 8, 128, 8, 512).transpose(0, 3, 2, 1, 4))
    w2R = np.ascontiguousarray(w2.reshape(8, 8, 4, 128, 1024).transpose(0, 1, 3, 2, 4))
    ltri = np.zeros((128, 128), dtype=np.float32)
    for p in range(128):
        ltri[:p, p] = 1.0
    m16 = np.zeros((128, 128), dtype=np.float32)
    for cc in range(16):
        for e in range(8):
            for cp in range(cc):
                m16[8 * cp + e, 8 * cc + e] = 1.0
    P8 = np.zeros((128, 128), dtype=np.float32)
    for cc in range(16):
        for e in range(8):
            P8[e, 8 * cc + e] = 1.0
    iotaF = np.zeros((128, 8), dtype=np.float32)
    for ff in range(8):
        iotaF[:, ff] = np.arange(128) + 128 * ff

    WqT = np.ascontiguousarray(f32(inputs["L0_attn_b_Wq"]).T)
    WkT = np.ascontiguousarray(f32(inputs["L0_attn_b_Wk"]).T)
    WvT = np.ascontiguousarray(f32(inputs["L0_attn_b_Wv"]).T)
    WoT = np.ascontiguousarray(f32(inputs["L0_attn_b_Wo"]).T)
    bqr = np.ascontiguousarray(f32(inputs["L0_attn_b_bq"]).reshape(8, 128).T)
    bkr = np.ascontiguousarray(f32(inputs["L0_attn_b_bk"]).reshape(8, 128).T)
    bor = np.ascontiguousarray(f32(inputs["L0_attn_b_bo"]).reshape(8, 128).T)
    bvbb = np.ascontiguousarray(np.broadcast_to(f32(inputs["L0_attn_b_bv"])[None, :],
                                                (128, D)), dtype=np.float32)

    if stage not in _NC_CACHE:
        _NC_CACHE[stage] = build_nc(stage)
    nc = _NC_CACHE[stage]

    in_maps = []
    own_rows = {}
    for c in range(8):
        b, h = c // 2, c % 2
        rows = np.concatenate([np.arange((2 * s + h) * 128, (2 * s + h + 1) * 128)
                               for s in range(NOWN)])
        own_rows[c] = rows
        im = {
            "x_row": x[b],
            "x_own": np.ascontiguousarray(x[b][rows]),
            "Wqk": Wqk,
            "bqk": bqk,
            "gam0": gam[0].reshape(DQK, 1), "bet0": bet[0].reshape(DQK, 1),
            "gam1": gam[1].reshape(DQK, 1), "bet1": bet[1].reshape(DQK, 1),
            "Wv_a": Wv,
            "bvbc_a": bvbc,
            "rel": rel,
            "Mh": Mh[h],
            "identc": ident,
        }
        if STAGE in ("x2", "x3", "full", "dbgF", "dbgB", "dbgF2", "dbgF3", "fullnc", "fullb1", "fullb2", "fullr", "fulle1", "fulle2", "fulle4", "fullc"):
            im.update({
                "gate": gate, "w1": w1, "b1r": b1r, "w2": w2, "b2bc": b2bc,
            })
        if STAGE in ("x2", "x3", "full", "dbgB", "fullc"):
            im.update({
                "w1R": w1R, "w2R": w2R, "b1f": b1, "b2f": b2,
                "ltri": ltri, "m16": m16, "P8": P8, "iotaF": iotaF,
            })
        if STAGE in ("x3", "full", "dbgB", "fullnc", "fullb1", "fullb2", "fullr", "fulle1", "fulle2", "fulle4"):
            im.update({
                "WqT": WqT, "WkT": WkT, "WvT": WvT, "WoT": WoT,
                "bqr": bqr, "bkr": bkr, "bor": bor, "bvbb": bvbb,
            })
        in_maps.append(im)

    res = run_bass_kernel_spmd(nc, in_maps, core_ids=list(range(8)))

    out = np.zeros((B, S, D), dtype=np.float32)
    for c in range(8):
        b = c // 2
        out[b][own_rows[c]] = res.results[c]["out"]
    return out
